# version 01
# baseline (speedup 1.0000x reference)
import math
from contextlib import ExitStack

import numpy as np
import ml_dtypes

import concourse.bass as bass
import concourse.mybir as mybir
from concourse.bass_utils import run_bass_kernel_spmd

F32 = mybir.dt.float32
BF16 = mybir.dt.bfloat16
AF = mybir.ActivationFunctionType
ALU = mybir.AluOpType
AX = mybir.AxisListType

B, S, C = 4, 4096, 1024
NT_OWN = 16
NT_ABS = 32
D_FF, N_EXP = 4096, 8
NEG = -1.0e9
CSHIFT = 20.0
EPS_LN = 1e-5
PHYSW = 2304
CHUNKS = [(c * 512, min((c + 1) * 512, PHYSW)) for c in range(5)]

ENGS = ("t", "v", "s", "g", "y")
TRUNC = 0
MOE_SKIP = set()


def _t0(kt):
    return kt // 2 + 1


def _c0(kt):
    return (_t0(kt) * 128) // 512


class Prog:
    def __init__(self):
        self.q = {e: [] for e in ENGS}
        self.cnt = {e: 0 for e in ENGS}
        self.waited = {e: {f: 0 for f in ENGS} for e in ENGS}

    def op(self, eng, fn, after=(), inc=1):
        waits = []
        for tok in after:
            if tok is None:
                continue
            e2, c2 = tok
            if self.waited[eng][e2] < c2:
                self.waited[eng][e2] = c2
                waits.append((e2, c2))
        self.cnt[eng] += inc
        self.q[eng].append((fn, waits, inc))
        return (eng, self.cnt[eng])

    def t(self, fn, after=(), inc=1):
        return self.op("t", fn, after, inc)

    def v(self, fn, after=(), inc=1):
        return self.op("v", fn, after, inc)

    def s(self, fn, after=(), inc=1):
        return self.op("s", fn, after, inc)

    def g(self, fn, after=(), inc=16):
        return self.op("g", fn, after, inc)

    def y(self, fn, after=(), inc=16):
        return self.op("y", fn, after, inc)

    def now(self, eng):
        return (eng, self.cnt[eng])

    def snap(self):
        return [self.now(e) for e in ENGS]


def _emit(nc, p, sems):
    with nc.Block() as blk:
        def mk(stream, eng_name):
            def body(eng):
                for fn, waits, inc in stream:
                    for (e2, c2) in waits:
                        eng.wait_ge(sems[e2], c2)
                    ins = fn(eng)
                    if inc:
                        ins.then_inc(sems[eng_name], inc)
            return body
        blk.tensor(mk(p.q["t"], "t"))
        blk.vector(mk(p.q["v"], "v"))
        blk.scalar(mk(p.q["s"], "s"))
        blk.gpsimd(mk(p.q["g"], "g"))
        blk.sync(mk(p.q["y"], "y"))



def host_constants(par):
    i = np.arange(128)[:, None]
    j = np.arange(128)[None, :]
    MDIAG = ((i > j) * NEG).astype(np.float32)
    MFULL = np.full((128, 128), NEG, np.float32)
    ZERO = np.zeros((128, 128), np.float32)
    cmask = np.zeros((2, 128, 128), np.float32)
    if par == 0:
        cmask[0] = MDIAG
        cmask[1] = MFULL
    else:
        cmask[0] = ZERO
        cmask[1] = MDIAG
    BMK = ((j < i) * NEG).astype(np.float32)
    BMQ = ((j > i) * NEG).astype(np.float32)
    lmask = np.zeros((2, 128, 256), np.float32)
    if par == 0:
        lmask[0, :, 0:128] = MFULL; lmask[0, :, 128:256] = ZERO
        lmask[1, :, 0:128] = BMK;   lmask[1, :, 128:256] = BMQ
    else:
        lmask[0, :, 0:128] = BMK;   lmask[0, :, 128:256] = BMQ
        lmask[1, :, 0:128] = ZERO;  lmask[1, :, 128:256] = MFULL
    return cmask, lmask


def rope_tables():
    d2 = C // 2
    theta = (1.0 / 10000.0 ** (np.arange(d2, dtype=np.float32) / np.float32(d2))).astype(np.float32)
    ang = np.arange(S, dtype=np.float32)[:, None] * theta[None, :]
    return np.cos(ang).astype(np.float32), np.sin(ang).astype(np.float32)



def build_l0(dbg=0):
    nc = bass.Bass("TRN2", target_bir_lowering=False, debug=False)
    p = Prog()
    ap = {}

    def din(name, shape):
        ap[name] = nc.dram_tensor(name, list(shape), F32, kind="ExternalInput").ap()

    din("xb", (S, C))
    din("xq", (NT_OWN * 128, C))
    din("rc", (S, C // 2)); din("rs", (S, C // 2))
    din("rcq", (NT_OWN * 128, C // 2)); din("rsq", (NT_OWN * 128, C // 2))
    din("wqT", (C, C)); din("wkT", (C, 256)); din("wvT", (C, 256)); din("woT", (C, C))
    din("wrT", (C, 8)); din("br_", (1, 8)); din("pw2", (1, 8))
    din("w1T", (N_EXP, C, D_FF)); din("b1_", (N_EXP, D_FF))
    din("w2T", (N_EXP, D_FF, C)); din("b2_", (N_EXP, C))
    din("cmask", (2, 128, 128))
    din("idm", (128, 128))
    din("un8", (8, 8 * 128))
    y0 = nc.dram_tensor("y0", [NT_OWN * 128, C], F32, kind="ExternalOutput").ap()
    att0_d = nc.dram_tensor("att0_d", [NT_OWN * 128, C], F32,
                            kind=("ExternalOutput" if dbg else "Internal")).ap()
    if dbg:
        dq = nc.dram_tensor("dq", [16, 64, PHYSW], F32, kind="ExternalOutput").ap()
        dk = nc.dram_tensor("dk", [4, 64, S], F32, kind="ExternalOutput").ap()
        dv = nc.dram_tensor("dv", [NT_ABS, 128, 264], F32, kind="ExternalOutput").ap()
        doh = nc.dram_tensor("doh", [NT_OWN, 128, 8], F32, kind="ExternalOutput").ap()
        dxn = nc.dram_tensor("dxn", [128, C], F32, kind="ExternalOutput").ap()
        dxr = nc.dram_tensor("dxr", [128, C], F32, kind="ExternalOutput").ap()
        dxf = nc.dram_tensor("dxf", [128, 8 * 512], F32, kind="ExternalOutput").ap()
        dst = nc.dram_tensor("dst", [128, 8], F32, kind="ExternalOutput").ap()

    ctx = ExitStack()
    sems = {e: ctx.enter_context(nc.semaphore("sem_" + e)) for e in ENGS}

    def sball(st, name, shape, dt):
        return st.enter_context(nc.sbuf_tensor(name, list(shape), dt))

    def psall(st, name, shape, dt=F32):
        return st.enter_context(nc.psum_tensor(name, list(shape), dt))

    ident = sball(ctx, "ident", (128, 128), BF16)
    identf = sball(ctx, "identf", (128, 128), F32)
    ones2 = sball(ctx, "ones2", (65, 128), F32)
    consts = sball(ctx, "consts", (128, 8), F32)
    oh_tm = [sball(ctx, f"oh{i}", (128, 8), F32) for i in range(NT_OWN)]
    nc.const_aps.register(F32, 0.125, consts[:, 2:3])
    nc.const_aps.register(F32, -0.5, consts[:, 3:4])
    nc.const_aps.register(F32, 1.0 / 1024.0, consts[:, 4:5])
    nc.const_aps.register(F32, 1.0, consts[:, 5:6])
    C20 = consts[:, 0:1]
    CEPS = consts[:, 1:2]

    ps_tr = psall(ctx, "ps_tr", (128, 1024), BF16)
    ps_a = [psall(ctx, "ps_a0", (128, 512)), psall(ctx, "ps_a1", (128, 512))]
    ps_b = [psall(ctx, "ps_b0", (128, 512)), psall(ctx, "ps_b1", (128, 512))]
    ps_c = [psall(ctx, "ps_c0", (128, 512)), psall(ctx, "ps_c1", (128, 512))]
    ps_d = psall(ctx, "ps_d", (128, 512))

    bank_free = {}

    tk_id = p.g(lambda e: e.dma_start(out=ident[:], in_=ap["idm"][:]))
    tk_idf = p.y(lambda e: e.dma_start(out=identf[:], in_=ap["idm"][:]))
    tk = p.v(lambda e: e.memset(consts[:], 0.0))
    tk = p.v(lambda e: e.memset(consts[:, 0:1], -CSHIFT), after=[tk])
    tk = p.v(lambda e: e.memset(consts[:, 1:2], EPS_LN), after=[tk])
    tk = p.v(lambda e: e.memset(consts[:, 2:3], 0.125), after=[tk])
    tk = p.v(lambda e: e.memset(consts[:, 3:4], -0.5), after=[tk])
    tk = p.v(lambda e: e.memset(consts[:, 4:5], 1.0 / 1024.0), after=[tk])
    tk = p.v(lambda e: e.memset(consts[:, 5:6], 1.0), after=[tk])
    tk_consts = tk
    tk_ones = p.v(lambda e: e.memset(ones2[:], 1.0), after=[tk_consts])

    war = {}

    def use(name, tok):
        war.setdefault(name, []).append(tok)

    def takew(name):
        return war.pop(name, [])

    def pop_bank(nm):
        return [bank_free.pop(nm)] if nm in bank_free else []

    sA = ExitStack()
    q_fm = [sball(sA, f"qfm{h}", (64, PHYSW), BF16) for h in range(16)]
    k_fm = [sball(sA, f"kfm{g}", (64, S), BF16) for g in range(4)]
    v_tm = [sball(sA, f"vtm{i}", (128, 264), BF16) for i in range(NT_ABS)]

    tk_qpad = {}
    for h in range(16):
        t1 = p.v(lambda e, h=h: e.memset(q_fm[h][:, 0:128], 0.0))
        t2 = p.v(lambda e, h=h: e.memset(q_fm[h][:, 2176:PHYSW], 0.0), after=[t1])
        tk_qpad[h] = t2

    sP = ExitStack()
    xb_t = [sball(sP, f"xbt{i}", (128, C), F32) for i in range(2)]
    rc_t = [sball(sP, f"rct{i}", (128, 512), F32) for i in range(2)]
    rs_t = [sball(sP, f"rst{i}", (128, 512), F32) for i in range(2)]
    xn_t = [sball(sP, f"xnt{i}", (128, C), F32) for i in range(2)]
    xr_t = [sball(sP, f"xrt{i}", (128, C), F32) for i in range(2)]
    st_t = [sball(sP, f"stt{i}", (128, 8), F32) for i in range(2)]
    xrb_t = [sball(sP, f"xrbt{i}", (128, C), BF16) for i in range(2)]
    xr_fm = sball(sP, "xr_fm", (128, 8 * 512), BF16)
    lo_fm = sball(sP, "lo_fm", (128, 8 * 128), BF16)
    hi_fm = sball(sP, "hi_fm", (128, 8 * 128), BF16)
    xnb_t = [sball(sP, f"xnbt{i}", (128, C), BF16) for i in range(2)]
    xlo_t = [sball(sP, f"xlot{i}", (128, C), BF16) for i in range(2)]
    wq_sb = sball(sP, "wq_sb", (128, 8 * C), BF16)
    wk_sb = sball(sP, "wk_sb", (128, 8 * 256), BF16)
    wv_sb = sball(sP, "wv_sb", (128, 8 * 256), BF16)
    wr_hi = sball(sP, "wr_hi", (128, 8 * 8), BF16)
    wr_lo = sball(sP, "wr_lo", (128, 8 * 8), BF16)
    wr_f32 = sball(sP, "wr_f32", (128, 8 * 8), F32)
    brb = sball(sP, "brb", (128, 8), F32)
    pwb = sball(sP, "pwb", (128, 8), F32)
    br_row = sball(sP, "br_row", (1, 8), F32)
    pw_row = sball(sP, "pw_row", (1, 8), F32)
    logit_t = [sball(sP, f"lgt{i}", (128, 8), F32) for i in range(2)]

    tk_wq = p.g(lambda e: e.dma_start(out=wq_sb[:].rearrange("p (c o) -> p c o", c=8),
                                      in_=ap["wqT"].rearrange("(c p) o -> p c o", p=128)))
    tk_wk = p.g(lambda e: e.dma_start(out=wk_sb[:].rearrange("p (c o) -> p c o", c=8),
                                      in_=ap["wkT"].rearrange("(c p) o -> p c o", p=128)))
    tk_wv = p.g(lambda e: e.dma_start(out=wv_sb[:].rearrange("p (c o) -> p c o", c=8),
                                      in_=ap["wvT"].rearrange("(c p) o -> p c o", p=128)))
    tk_wrf = p.g(lambda e: e.dma_start(out=wr_f32[:].rearrange("p (c o) -> p c o", c=8),
                                       in_=ap["wrT"].rearrange("(c p) o -> p c o", p=128)))
    tk_wrh = p.v(lambda e: e.tensor_copy(wr_hi[:], wr_f32[:]), after=[tk_wrf])
    tk_wrl = p.v(lambda e: e.tensor_sub(wr_lo[:], wr_f32[:], wr_hi[:]), after=[tk_wrh])
    tk_brr = p.y(lambda e: e.dma_start(out=br_row[:], in_=ap["br_"][:]))
    tk_pwr = p.y(lambda e: e.dma_start(out=pw_row[:], in_=ap["pw2"][:]))
    tk_bc = p.t(lambda e: e.matmul(ps_d[:, 0:8], ones2[0:1, :], br_row[:], start=True, stop=True),
                after=[tk_brr, tk_ones])
    tk_brb = p.v(lambda e: e.tensor_copy(brb[:], ps_d[:, 0:8]), after=[tk_bc])
    tk_bc2 = p.t(lambda e: e.matmul(ps_d[:, 8:16], ones2[0:1, :], pw_row[:], start=True, stop=True),
                 after=[tk_pwr, tk_brb])
    tk_pwb = p.v(lambda e: e.tensor_copy(pwb[:], ps_d[:, 8:16]), after=[tk_bc2])
    bank_free["d"] = tk_pwb

    def ln_chain(xsrc_tok, xbuf, xnbuf, stbuf, extra_after=()):
        t1 = p.v(lambda e: e.reduce_sum(stbuf[:, 0:1], xbuf[:], axis=AX.X),
                 after=[xsrc_tok, tk_consts, *extra_after])
        t2 = p.s(lambda e: e.activation(out=xnbuf[:], in_=xbuf[:], func=AF.Square,
                                        accum_out=stbuf[:, 2:3]),
                 after=[xsrc_tok, tk_consts, *extra_after])
        t3 = p.v(lambda e: e.tensor_scalar_mul(stbuf[:, 1:2], stbuf[:, 0:1], 1.0 / 1024.0), after=[t1])
        t4 = p.v(lambda e: e.tensor_scalar_mul(stbuf[:, 3:4], stbuf[:, 2:3], 1.0 / 1024.0), after=[t2])
        t5 = p.v(lambda e: e.tensor_mul(stbuf[:, 4:5], stbuf[:, 1:2], stbuf[:, 1:2]), after=[t3])
        t5b = p.v(lambda e: e.tensor_sub(stbuf[:, 5:6], stbuf[:, 3:4], stbuf[:, 4:5]), after=[t4, t5])
        t6 = p.s(lambda e: e.activation(out=stbuf[:, 6:7], in_=stbuf[:, 5:6], func=AF.Ln,
                                        bias=CEPS), after=[t5b])
        t7 = p.s(lambda e: e.activation(out=stbuf[:, 7:8], in_=stbuf[:, 6:7], func=AF.Exp,
                                        scale=-0.5), after=[t6])
        t8a = p.v(lambda e: e.tensor_single_scalar(xnbuf[:], xbuf[:], stbuf[:, 1:2], ALU.subtract),
                  after=[t7, t2])
        t8 = p.v(lambda e: e.tensor_single_scalar(xnbuf[:], xnbuf[:], stbuf[:, 7:8], ALU.mult),
                 after=[t8a])
        return t8

    def rope_chain(xn_tok, xnbuf, xrbuf, scrbuf, rcb, rsb, rope_tok, scr_ok):
        xe = xnbuf[:, 0:C:2]
        xo = xnbuf[:, 1:C:2]
        a1 = p.v(lambda e: e.tensor_mul(xrbuf[:, 0:512], xe, rcb[:]), after=[xn_tok, rope_tok])
        a2 = p.v(lambda e: e.tensor_mul(scrbuf[:, 0:512], xo, rsb[:]), after=[xn_tok, rope_tok, scr_ok])
        a3 = p.v(lambda e: e.tensor_sub(xrbuf[:, 0:512], xrbuf[:, 0:512], scrbuf[:, 0:512]), after=[a1, a2])
        b1 = p.g(lambda e: e.tensor_mul(xrbuf[:, 512:C], xe, rsb[:]), after=[xn_tok, rope_tok], inc=1)
        b2 = p.g(lambda e: e.tensor_mul(scrbuf[:, 512:C], xo, rcb[:]), after=[xn_tok, rope_tok, scr_ok], inc=1)
        b3 = p.g(lambda e: e.tensor_add(xrbuf[:, 512:C], xrbuf[:, 512:C], scrbuf[:, 512:C]), after=[b1, b2], inc=1)
        return a3, b3

    for t in range(min(TRUNC, NT_ABS) if TRUNC else NT_ABS):
        i = t % 2
        ld = p.y(lambda e, t=t, i=i: e.dma_start(out=xb_t[i][:], in_=ap["xb"][t * 128:(t + 1) * 128, :]),
                 after=takew(f"xb{i}"))
        lc = p.y(lambda e, t=t, i=i: e.dma_start(out=rc_t[i][:], in_=ap["rc"][t * 128:(t + 1) * 128, :]),
                 after=takew(f"rc{i}"))
        ls = p.y(lambda e, t=t, i=i: e.dma_start(out=rs_t[i][:], in_=ap["rs"][t * 128:(t + 1) * 128, :]),
                 after=takew(f"rs{i}"))
        xn_tok = ln_chain(ld, xb_t[i], xn_t[i], st_t[i], extra_after=takew(f"ln{i}"))
        ra, rb = rope_chain(xn_tok, xn_t[i], xr_t[i], xb_t[i], rc_t[i], rs_t[i], lc, xn_tok)
        use(f"rc{i}", ra); use(f"rs{i}", rb); use(f"rc{i}", rb); use(f"rs{i}", ra)
        use(f"xb{i}", ra); use(f"xb{i}", rb)
        cast = p.v(lambda e, i=i: e.tensor_copy(xrb_t[i][:], xr_t[i][:]), after=[ra, rb])
        if dbg and t == 0:
            dmp1 = p.y(lambda e: e.dma_start(out=dxn[:], in_=xn_t[0][:]), after=[xn_tok, ra, rb])
            dmp2 = p.y(lambda e: e.dma_start(out=dxr[:], in_=xr_t[0][:]), after=[ra, rb])
            dmp4 = p.y(lambda e: e.dma_start(out=dst[:], in_=st_t[0][:]), after=[xn_tok])
            use("ln0", dmp1); use("ln0", dmp2); use("ln0", dmp4); use("xb0", dmp1)
        use(f"ln{i}", ra); use(f"ln{i}", rb); use(f"ln{i}", cast)
        trs = None
        for cc in range(8):
            trs = p.t(lambda e, cc=cc, i=i: e.transpose(
                ps_tr[:, cc * 128:(cc + 1) * 128], xrb_t[i][:, cc * 128:(cc + 1) * 128], ident[:]),
                after=([cast, tk_id] + pop_bank("tr") if cc == 0 else []), inc=(1 if cc == 7 else 0))
        cpy = None
        for cc in range(8):
            cpy = p.v(lambda e, cc=cc, t=t: e.tensor_copy(
                xr_fm[:, cc * 512 + (t % 4) * 128: cc * 512 + (t % 4) * 128 + 128],
                ps_tr[:, cc * 128:(cc + 1) * 128]),
                after=[trs] + (takew(f"xrfm{t % 4}") if cc == 0 else []))
        bank_free["tr"] = cpy
        if dbg and t == 3:
            dmp3 = p.g(lambda e: e.dma_start(out=dxf[:], in_=xr_fm[:]), after=[cpy])
            for sl in range(4):
                use(f"xrfm{sl}", dmp3)
        use(f"xb{i}", cast)
        pv = ps_a[t % 2]
        aft = [cpy, tk_wv] + pop_bank(f"a{t % 2}")
        vmm = None
        for cc in range(8):
            vmm = p.t(lambda e, cc=cc, t=t, pv=pv: e.matmul(
                pv[:, 0:256], xr_fm[:, cc * 512 + (t % 4) * 128: cc * 512 + (t % 4) * 128 + 128],
                wv_sb[:, cc * 256:(cc + 1) * 256], start=(cc == 0), stop=(cc == 7)),
                after=(aft if cc == 0 else []), inc=(1 if cc == 7 else 0))
        vz = p.v(lambda e, t=t: e.memset(v_tm[t][:], 0.0))
        vo = p.v(lambda e, t=t: e.memset(
            v_tm[t][:, 0:260].rearrange("p (g w) -> p g w", w=65)[:, :, 64:65], 1.0), after=[vz])
        ve = p.v(lambda e, t=t, pv=pv: e.tensor_copy(
            v_tm[t][:, 0:260].rearrange("p (g w) -> p g w", w=65)[:, :, 0:64],
            pv[:, 0:256].rearrange("p (g w) -> p g w", w=64)), after=[vmm, vo])
        bank_free[f"a{t % 2}"] = ve
        use(f"xrfm{t % 4}", vmm)
        if t % 4 == 3:
            g0 = t - 3
            for g in range(4):
                pk = ps_b[g % 2]
                kaft = [cpy, tk_wk] + pop_bank(f"b{g % 2}")
                kmm = None
                for cc in range(8):
                    kmm = p.t(lambda e, cc=cc, g=g, pk=pk: e.matmul(
                        pk[0:64, 0:512], wk_sb[:, cc * 256 + g * 64: cc * 256 + g * 64 + 64],
                        xr_fm[:, cc * 512:(cc + 1) * 512], start=(cc == 0), stop=(cc == 7)),
                        after=(kaft if cc == 0 else []), inc=(1 if cc == 7 else 0))
                ke = p.s(lambda e, g=g, g0=g0, pk=pk: e.activation(
                    out=k_fm[g][:, g0 * 128:(g0 + 4) * 128], in_=pk[0:64, 0:512], func=AF.Copy),
                    after=[kmm])
                bank_free[f"b{g % 2}"] = ke
                for sl in range(4):
                    use(f"xrfm{sl}", kmm)
    kv_tok = p.snap()
    if TRUNC and TRUNC <= 32:
        p.op("y", lambda e: e.nop(), after=[p.now("y")], inc=0)
        sP.close()
        sA.close()
        _emit(nc, p, sems)
        ctx.close()
        return nc

    for i2 in range(NT_OWN):
        i = i2 % 2
        ld = p.y(lambda e, i2=i2, i=i: e.dma_start(out=xb_t[i][:], in_=ap["xq"][i2 * 128:(i2 + 1) * 128, :]),
                 after=takew(f"xb{i}"))
        lc = p.y(lambda e, i2=i2, i=i: e.dma_start(out=rc_t[i][:], in_=ap["rcq"][i2 * 128:(i2 + 1) * 128, :]),
                 after=takew(f"rc{i}"))
        ls = p.y(lambda e, i2=i2, i=i: e.dma_start(out=rs_t[i][:], in_=ap["rsq"][i2 * 128:(i2 + 1) * 128, :]),
                 after=takew(f"rs{i}"))
        xn_tok = ln_chain(ld, xb_t[i], xn_t[i], st_t[i], extra_after=takew(f"ln{i}"))
        hi = p.v(lambda e, i=i: e.tensor_copy(xnb_t[i][:], xn_t[i][:]), after=[xn_tok] + takew(f"xnb{i}"))
        lo = p.v(lambda e, i=i: e.tensor_sub(xlo_t[i][:], xn_t[i][:], xnb_t[i][:]), after=[hi] + takew(f"xlo{i}"))
        ra, rb = rope_chain(xn_tok, xn_t[i], xr_t[i], xb_t[i], rc_t[i], rs_t[i], lc, xn_tok)
        use(f"rc{i}", ra); use(f"rs{i}", rb); use(f"rc{i}", rb); use(f"rs{i}", ra)
        use(f"xb{i}", ra); use(f"xb{i}", rb)
        cast = p.v(lambda e, i=i: e.tensor_copy(xrb_t[i][:], xr_t[i][:]), after=[ra, rb, lo])
        use(f"ln{i}", ra); use(f"ln{i}", rb); use(f"ln{i}", cast)
        trs = None
        for cc in range(8):
            trs = p.t(lambda e, cc=cc, i=i: e.transpose(
                ps_tr[:, cc * 128:(cc + 1) * 128], xrb_t[i][:, cc * 128:(cc + 1) * 128], ident[:]),
                after=([cast] + pop_bank("tr") if cc == 0 else []), inc=(1 if cc == 7 else 0))
        cpy = None
        for cc in range(8):
            cpy = p.v(lambda e, cc=cc, i2=i2: e.tensor_copy(
                xr_fm[:, cc * 512 + (i2 % 4) * 128: cc * 512 + (i2 % 4) * 128 + 128],
                ps_tr[:, cc * 128:(cc + 1) * 128]),
                after=[trs] + (takew(f"xrfm{i2 % 4}") if cc == 0 else []))
        bank_free["tr"] = cpy
        trh = None
        for cc in range(8):
            trh = p.t(lambda e, cc=cc, i=i: e.transpose(
                ps_tr[:, cc * 128:(cc + 1) * 128], xnb_t[i][:, cc * 128:(cc + 1) * 128], ident[:]),
                after=([hi] + pop_bank("tr") if cc == 0 else []), inc=(1 if cc == 7 else 0))
        use(f"xnb{i}", trh)
        cph = None
        for cc in range(8):
            cph = p.v(lambda e, cc=cc: e.tensor_copy(
                hi_fm[:, cc * 128:(cc + 1) * 128], ps_tr[:, cc * 128:(cc + 1) * 128]),
                after=[trh] + (takew("hifm") if cc == 0 else []))
        bank_free["tr"] = cph
        trl = None
        for cc in range(8):
            trl = p.t(lambda e, cc=cc, i=i: e.transpose(
                ps_tr[:, cc * 128:(cc + 1) * 128], xlo_t[i][:, cc * 128:(cc + 1) * 128], ident[:]),
                after=([lo] + pop_bank("tr") if cc == 0 else []), inc=(1 if cc == 7 else 0))
        use(f"xlo{i}", trl)
        cpl = None
        for cc in range(8):
            cpl = p.v(lambda e, cc=cc: e.tensor_copy(
                lo_fm[:, cc * 128:(cc + 1) * 128], ps_tr[:, cc * 128:(cc + 1) * 128]),
                after=[trl] + (takew("lofm") if cc == 0 else []))
        bank_free["tr"] = cpl
        raft = [cph, cpl, tk_wrh, tk_wrl] + pop_bank("d")
        rmm = None
        n_r = 0
        for (lhs_buf, w_buf) in ((hi_fm, wr_hi), (hi_fm, wr_lo), (lo_fm, wr_hi)):
            for cc in range(8):
                n_r += 1
                rmm = p.t(lambda e, cc=cc, lhs_buf=lhs_buf, w_buf=w_buf, first=(n_r == 1), last=(n_r == 24):
                          e.matmul(ps_d[:, 0:8], lhs_buf[:, cc * 128:(cc + 1) * 128],
                                   w_buf[:, cc * 8:(cc + 1) * 8], start=first, stop=last),
                          after=(raft if n_r == 1 else []), inc=(1 if n_r == 24 else 0))
        use("hifm", rmm); use("lofm", rmm)
        lg = logit_t[i]
        l1 = p.v(lambda e, lg=lg: e.tensor_add(lg[:], ps_d[:, 0:8], brb[:]),
                 after=[rmm, tk_brb] + takew(f"lg{i}"))
        bank_free["d"] = l1
        m1 = p.v(lambda e, i=i, lg=lg: e.reduce_max(st_t[i][:, 6:7], lg[:], axis=AX.X), after=[l1])
        m2 = p.v(lambda e, i2=i2, i=i, lg=lg: e.tensor_single_scalar(oh_tm[i2][:], lg[:], st_t[i][:, 6:7], ALU.is_ge),
                 after=[m1])
        m3 = p.v(lambda e, i2=i2: e.tensor_mul(oh_tm[i2][:], oh_tm[i2][:], pwb[:]), after=[m2, tk_pwb])
        m4 = p.v(lambda e, i2=i2, i=i: e.reduce_max(st_t[i][:, 6:7], oh_tm[i2][:], axis=AX.X), after=[m3])
        m5 = p.v(lambda e, i2=i2, i=i: e.tensor_single_scalar(oh_tm[i2][:], oh_tm[i2][:], st_t[i][:, 6:7], ALU.is_ge),
                 after=[m4])
        use(f"lg{i}", m2)
        use(f"ln{i}", m5)
        if i2 % 4 == 3:
            ch = i2 // 4
            for h in range(16):
                pq = ps_b[h % 2]
                qaft = [cpy, tk_wq] + pop_bank(f"b{h % 2}")
                qmm = None
                for cc in range(8):
                    qmm = p.t(lambda e, cc=cc, h=h, pq=pq: e.matmul(
                        pq[0:64, 0:512], wq_sb[:, cc * 1024 + h * 64: cc * 1024 + h * 64 + 64],
                        xr_fm[:, cc * 512:(cc + 1) * 512], start=(cc == 0), stop=(cc == 7)),
                        after=(qaft if cc == 0 else []), inc=(1 if cc == 7 else 0))
                qe = p.s(lambda e, h=h, ch=ch, pq=pq: e.activation(
                    out=q_fm[h][:, 128 + ch * 512: 128 + (ch + 1) * 512], in_=pq[0:64, 0:512], func=AF.Copy),
                    after=[qmm, tk_qpad[h]])
                bank_free[f"b{h % 2}"] = qe
                for sl in range(4):
                    use(f"xrfm{sl}", qmm)
    q_tok = p.snap()
    if TRUNC == 33:
        p.op("y", lambda e: e.nop(), after=[p.now("y")], inc=0)
        sP.close()
        sA.close()
        _emit(nc, p, sems)
        ctx.close()
        return nc

    prep_done = p.snap()
    sP.close()

    cmask_sb = sball(sA, "cmask_sb", (128, 2 * 128), F32)
    wo_sb = [sball(sA, f"wo{h}", (64, C), BF16) for h in range(16)]
    pt_sb = [sball(sA, f"pt{i}", (128, 512), BF16) for i in range(2)]
    att_h = [[sball(sA, f"ath{h}_{cp}", (64, 512), BF16) for cp in range(2)] for h in range(16)]
    linv_sb = sball(sA, "linv_sb", (128, 512), F32)
    linvB = sball(sA, "linvB", (128, 512), F32)
    oout = [sball(sA, f"oout{i}", (128, 512), F32) for i in range(2)]

    tk_cm = p.y(lambda e: e.dma_start(out=cmask_sb[:].rearrange("p (k q) -> p k q", k=2),
                                      in_=ap["cmask"].rearrange("k p q -> p k q")), after=prep_done)
    tk_wo = None
    for h in range(16):
        tk_wo = p.g(lambda e, h=h: e.dma_start(out=wo_sb[h][:], in_=ap["woT"][h * 64:(h + 1) * 64, :]),
                    after=(prep_done if h == 0 else []))

    for cch in range(5):
        c_lo, c_hi = CHUNKS[cch]
        oproj_lhs = []
        for h in range(16):
            g = h // 4
            kts = [kt for kt in range(NT_ABS) if _c0(kt) <= cch]
            pb = ps_b[h % 2]
            evac_dep = pop_bank(f"b{h % 2}")
            att_first = True
            pv = None
            for kt in kts:
                cs = max(c_lo, _t0(kt) * 128)
                off = cs - c_lo
                N = c_hi - cs
                pa = ps_a[kt % 2]
                saft = (kv_tok + q_tok if (cch == 0 and h == 0 and kt == kts[0]) else []) + pop_bank(f"a{kt % 2}")
                smm = p.t(lambda e, g=g, h=h, kt=kt, cs=cs, N=N, off=off, pa=pa: e.matmul(
                    pa[0:128, off:off + N],
                    k_fm[g][:, kt * 128:(kt + 1) * 128],
                    q_fm[h][:, cs:cs + N],
                    start=True, stop=True), after=saft)
                mtok = smm
                if cch == _c0(kt):
                    mtok = p.v(lambda e, kt=kt, off=off, pa=pa: e.tensor_add(
                        pa[:, off:off + 128], pa[:, off:off + 128],
                        cmask_sb[:, (kt % 2) * 128:(kt % 2) * 128 + 128]), after=[smm, tk_cm])
                ex = p.s(lambda e, off=off, N=N, kt=kt, pa=pa: e.activation(
                    out=pt_sb[kt % 2][:, off:off + N], in_=pa[:, off:off + N], func=AF.Exp,
                    scale=0.125, bias=C20), after=[mtok] + takew(f"pt{kt % 2}"))
                bank_free[f"a{kt % 2}"] = ex
                pvaft = [ex] + (evac_dep if att_first else [])
                pv = p.t(lambda e, kt=kt, g=g, off=off, N=N, first=att_first, last=(kt == kts[-1]), pb=pb:
                         e.matmul(pb[0:65, off:off + N], v_tm[kt][:, g * 65: g * 65 + 65],
                                  pt_sb[kt % 2][:, off:off + N], start=first, stop=last),
                         after=pvaft, inc=(1 if kt == kts[-1] else 0))
                use(f"pt{kt % 2}", pv)
                att_first = False
            rc1 = p.v(lambda e, pb=pb: e.reciprocal(linv_sb[64:65, :], pb[64:65, 0:512]),
                      after=[pv] + takew("linv"))
            bc = p.t(lambda e: e.matmul(ps_d[:, 0:512], ones2[64:65, :], linv_sb[64:65, 0:512],
                                        start=True, stop=True), after=[rc1] + pop_bank("d"))
            use("linv", bc)
            lb = p.s(lambda e: e.activation(out=linvB[:], in_=ps_d[:, 0:512], func=AF.Copy),
                     after=[bc] + takew("linvB"))
            bank_free["d"] = lb
            ev = p.v(lambda e, h=h, cch=cch, pb=pb: e.tensor_mul(
                att_h[h][cch % 2][:], pb[0:64, 0:512], linvB[0:64, :]),
                after=[lb] + takew(f"ath{h}{cch % 2}"))
            use("linvB", ev)
            bank_free[f"b{h % 2}"] = ev
            oproj_lhs.append(ev)
        for psub in range(cch * 4, cch * 4 + 4):
            if not (1 <= psub <= 16):
                continue
            i2 = psub - 1
            offp = psub * 128 - c_lo
            for m2 in range(2):
                pc = ps_c[m2]
                oaft = oproj_lhs + [tk_wo] + pop_bank(f"c{m2}")
                omm = None
                for h in range(16):
                    omm = p.t(lambda e, h=h, m2=m2, offp=offp, cch=cch, pc=pc: e.matmul(
                        pc[:, 0:512],
                        att_h[h][cch % 2][:, offp:offp + 128],
                        wo_sb[h][:, m2 * 512:(m2 + 1) * 512],
                        start=(h == 0), stop=(h == 15)),
                        after=(oaft if h == 0 else []), inc=(1 if h == 15 else 0))
                for h in range(16):
                    use(f"ath{h}{cch % 2}", omm)
                ocp = p.s(lambda e, m2=m2, pc=pc: e.activation(
                    out=oout[m2][:], in_=pc[:, 0:512], func=AF.Copy),
                    after=[omm] + takew(f"oout{m2}"))
                bank_free[f"c{m2}"] = ocp
                ost = p.y(lambda e, i2=i2, m2=m2: e.dma_start(
                    out=att0_d[i2 * 128:(i2 + 1) * 128, m2 * 512:(m2 + 1) * 512],
                    in_=oout[m2][:]), after=[ocp])
                use(f"oout{m2}", ost)

    if dbg:
        for h in range(16):
            p.g(lambda e, h=h: e.dma_start(out=dq[h], in_=q_fm[h][:]), after=p.snap())
        for g in range(4):
            p.g(lambda e, g=g: e.dma_start(out=dk[g], in_=k_fm[g][:]))
        for t in range(NT_ABS):
            p.g(lambda e, t=t: e.dma_start(out=dv[t], in_=v_tm[t][:]))
        for i2 in range(NT_OWN):
            p.y(lambda e, i2=i2: e.dma_start(out=doh[i2], in_=oh_tm[i2][:]))
    attn_done = p.snap()
    if TRUNC == 34:
        p.op("y", lambda e: e.nop(), after=[p.now("y")], inc=0)
        sA.close()
        _emit(nc, p, sems)
        ctx.close()
        return nc
    sA.close()

    sB = ExitStack()
    acc = [sball(sB, f"acc{i}", (128, C), F32) for i in range(NT_OWN)]
    ao_t = [sball(sB, f"aot{i}", (128, C), F32) for i in range(2)]
    xq_hi = sball(sB, "xq_hi", (128, 8 * NT_OWN * 128), BF16)
    bxb_t = [sball(sB, f"bxbt{i}", (128, C), F32) for i in range(2)]
    bxn_t = [sball(sB, f"bxnt{i}", (128, C), F32) for i in range(2)]
    bst_t = [sball(sB, f"bstt{i}", (128, 8), F32) for i in range(2)]
    bxnb_t = [sball(sB, f"bxnbt{i}", (128, C), BF16) for i in range(2)]
    h_sb = [sball(sB, f"hsb{i}", (128, 2048), BF16) for i in range(8)]
    w1_sb = [sball(sB, f"w1sb{i}", (128, 8 * 128), BF16) for i in range(2)]
    w2_sb = [sball(sB, f"w2sb{i}", (128, 1024), BF16) for i in range(8)]
    maskB = sball(sB, "maskB", (128, 2048), BF16)
    ohfm = sball(sB, "ohfm", (8, 2048), BF16)
    un8 = sball(sB, "un8_sb", (8, 8 * 128), BF16)
    b1T = [sball(sB, f"b1T{i}", (128, 32), F32) for i in range(2)]
    b1r = [sball(sB, f"b1r{i}", (32, 128), F32) for i in range(2)]
    b2_sb = sball(sB, "b2_sb", (8, C), BF16)
    hrelu = [sball(sB, f"hrelu{i}", (128, 512), F32) for i in range(2)]

    tk_b2 = p.g(lambda e: e.dma_start(out=b2_sb[:], in_=ap["b2_"][:, :]), after=attn_done)
    tk_u8 = p.g(lambda e: e.dma_start(out=un8[:], in_=ap["un8"][:]))

    moe_prep = []
    for i2 in range(NT_OWN):
        i = i2 % 2
        ld = p.y(lambda e, i2=i2, i=i: e.dma_start(out=bxb_t[i][:], in_=ap["xq"][i2 * 128:(i2 + 1) * 128, :]),
                 after=(attn_done if i2 < 2 else []) + takew(f"bxb{i}"))
        ldc = p.y(lambda e, i2=i2: e.dma_start(out=acc[i2][:], in_=ap["xq"][i2 * 128:(i2 + 1) * 128, :]),
                  after=(attn_done if i2 == 0 else []))
        lda = p.y(lambda e, i2=i2, i=i: e.dma_start(out=ao_t[i][:], in_=att0_d[i2 * 128:(i2 + 1) * 128, :]),
                  after=takew(f"ao{i}"))
        aadd = p.v(lambda e, i2=i2, i=i: e.tensor_add(acc[i2][:], acc[i2][:], ao_t[i][:]),
                   after=[ldc, lda])
        use(f"ao{i}", aadd)
        use(f"acc{i2}", aadd)
        if "ln" in MOE_SKIP:
            continue
        bxn_tok = ln_chain(ld, bxb_t[i], bxn_t[i], bst_t[i], extra_after=takew(f"bln{i}"))
        use(f"bxb{i}", bxn_tok)
        hi = p.v(lambda e, i=i: e.tensor_copy(bxnb_t[i][:], bxn_t[i][:]), after=[bxn_tok] + takew(f"bxnb{i}"))
        use(f"bln{i}", hi)
        if "tr" in MOE_SKIP:
            moe_prep.append(hi)
            continue
        trs = None
        for cc in range(8):
            trs = p.t(lambda e, cc=cc, i=i: e.transpose(
                ps_tr[:, cc * 128:(cc + 1) * 128], bxnb_t[i][:, cc * 128:(cc + 1) * 128], ident[:]),
                after=(([hi] + pop_bank("tr") + (attn_done if i2 == 0 else [])) if cc == 0 else []),
                inc=(1 if cc == 7 else 0))
        use(f"bxnb{i}", trs)
        cpy = None
        for cc in range(8):
            cpy = p.v(lambda e, cc=cc, i2=i2: e.tensor_copy(
                xq_hi[:, cc * 2048 + i2 * 128: cc * 2048 + i2 * 128 + 128],
                ps_tr[:, cc * 128:(cc + 1) * 128]), after=[trs])
        bank_free["tr"] = cpy
        moe_prep.append(cpy)
        tro = p.t(lambda e, i2=i2: e.transpose(ps_a[0][0:8, 0:128], oh_tm[i2][:], identf[:]),
                  after=[tk_idf] + (attn_done if i2 == 0 else []) + pop_bank("a0"))
        cpo = p.v(lambda e, i2=i2: e.tensor_copy(ohfm[0:8, i2 * 128:(i2 + 1) * 128], ps_a[0][0:8, 0:128]),
                  after=[tro])
        bank_free["a0"] = cpo
        moe_prep.append(cpo)
    xq_tok = moe_prep[-1] if moe_prep else p.snap()[1]
    oh_done = p.now("v")
    if TRUNC == 35:
        p.op("y", lambda e: e.nop(), after=[p.now("y")], inc=0)
        sB.close()
        _emit(nc, p, sems)
        ctx.close()
        return nc

    for e_i in range(1 if TRUNC == 36 else N_EXP):
        mb_toks = []
        for tc in range(4):
            mba = pop_bank("d") + [oh_done, tk_u8]
            mb1 = p.t(lambda e, e_i=e_i, tc=tc: e.matmul(
                ps_d[:, 0:512], un8[0:8, e_i * 128:(e_i + 1) * 128],
                ohfm[0:8, tc * 512:(tc + 1) * 512],
                start=True, stop=True), after=mba)
            mb2 = p.v(lambda e, tc=tc: e.tensor_copy(maskB[:, tc * 512:(tc + 1) * 512], ps_d[:, 0:512]),
                      after=[mb1] + takew("maskB"))
            bank_free["d"] = mb2
            mb_toks.append(mb2)
        bi = e_i % 2
        lb1 = p.y(lambda e, e_i=e_i, bi=bi: e.dma_start(
            out=b1r[bi][:], in_=ap["b1_"][e_i].rearrange("(a b) -> a b", a=32)), after=takew(f"b1r{bi}"))
        trb = p.t(lambda e, bi=bi: e.transpose(ps_a[0][0:128, 0:32], b1r[bi][:], identf[0:32, 0:32]),
                  after=[lb1] + pop_bank("a0"))
        use(f"b1r{bi}", trb)
        cb1 = p.v(lambda e, bi=bi: e.tensor_copy(b1T[bi][:], ps_a[0][0:128, 0:32]),
                  after=[trb] + takew(f"b1T{bi}"))
        bank_free["a0"] = cb1
        for grp in range(4):
            h_toks = []
            lw2 = None
            for ftl in range(8):
                ft = grp * 8 + ftl
                wi = ft % 2
                lw1 = p.g(lambda e, e_i=e_i, ft=ft, wi=wi: e.dma_start(
                    out=w1_sb[wi][:].rearrange("p (c f) -> p c f", c=8),
                    in_=ap["w1T"][e_i].rearrange("(c p) f -> p c f", p=128)[:, :, ft * 128:(ft + 1) * 128]),
                    after=takew(f"w1{wi}"))
                lw2 = p.g(lambda e, e_i=e_i, ft=ft, ftl=ftl: e.dma_start(
                    out=w2_sb[ftl][:], in_=ap["w2T"][e_i][ft * 128:(ft + 1) * 128, :]),
                    after=takew(f"w2{ftl}"))
                for tc in range(4):
                    pa = ps_a[tc % 2]
                    haft = [lw1, xq_tok, cb1] + pop_bank(f"a{tc % 2}")
                    hmm = None
                    for cc in range(8):
                        hmm = p.t(lambda e, cc=cc, tc=tc, wi=wi, pa=pa: e.matmul(
                            pa[:, 0:512], w1_sb[wi][:, cc * 128:(cc + 1) * 128],
                            xq_hi[:, cc * 2048 + tc * 512: cc * 2048 + (tc + 1) * 512],
                            start=(cc == 0), stop=(cc == 7)),
                            after=(haft if cc == 0 else []), inc=(1 if cc == 7 else 0))
                    use(f"w1{wi}", hmm)
                    hr = p.s(lambda e, tc=tc, ft=ft, bi=bi, pa=pa: e.activation(
                        out=hrelu[tc % 2][:], in_=pa[:, 0:512], func=AF.Relu,
                        bias=b1T[bi][:, (ft % 32):(ft % 32) + 1]),
                        after=[hmm] + takew(f"hrelu{tc % 2}"))
                    bank_free[f"a{tc % 2}"] = hr
                    use(f"b1T{bi}", hr)
                    hm = p.v(lambda e, tc=tc, ftl=ftl: e.tensor_mul(
                        h_sb[ftl][:, tc * 512:(tc + 1) * 512], hrelu[tc % 2][:],
                        maskB[:, tc * 512:(tc + 1) * 512]),
                        after=[hr, mb_toks[tc]] + takew(f"hsb{ftl}_{tc}"))
                    use("maskB", hm)
                    use(f"hrelu{tc % 2}", hm)
                    h_toks.append(hm)
            for ts in range(NT_OWN):
                for d2 in range(2):
                    pcx = [ps_b, ps_c][d2][ts % 2]
                    nm = f"{'b' if d2 == 0 else 'c'}{ts % 2}"
                    yaft = h_toks + [lw2] + pop_bank(nm)
                    first_mm = True
                    ymm = None
                    if e_i == 0 and grp == 0:
                        ymm = p.t(lambda e, ts=ts, d2=d2, pcx=pcx: e.matmul(
                            pcx[:, 0:512], ohfm[0:8, ts * 128:(ts + 1) * 128],
                            b2_sb[0:8, d2 * 512:(d2 + 1) * 512], start=True, stop=False),
                            after=yaft + [tk_b2], inc=0)
                        first_mm = False
                        yaft = []
                    for ftl in range(8):
                        ymm = p.t(lambda e, ftl=ftl, ts=ts, d2=d2, first=first_mm, pcx=pcx: e.matmul(
                            pcx[:, 0:512], h_sb[ftl][:, ts * 128:(ts + 1) * 128],
                            w2_sb[ftl][:, d2 * 512:(d2 + 1) * 512],
                            start=first, stop=(ftl == 7)),
                            after=(yaft if (first_mm and ftl == 0) else []), inc=(1 if ftl == 7 else 0))
                        first_mm = False
                    for ftl in range(8):
                        use(f"w2{ftl}", ymm)
                        use(f"hsb{ftl}_{ts // 4}", ymm)
                    yadd = p.v(lambda e, ts=ts, d2=d2, pcx=pcx: e.tensor_add(
                        acc[ts][:, d2 * 512:(d2 + 1) * 512], acc[ts][:, d2 * 512:(d2 + 1) * 512],
                        pcx[:, 0:512]), after=[ymm] + takew(f"acc{ts}")[-1:])
                    bank_free[nm] = yadd
                    use(f"acc{ts}", yadd)

    for i2 in range(NT_OWN):
        p.y(lambda e, i2=i2: e.dma_start(out=y0[i2 * 128:(i2 + 1) * 128, :], in_=acc[i2][:]),
            after=takew(f"acc{i2}")[-1:])

    p.op("y", lambda e: e.nop(), after=[p.now("y")], inc=0)

    sB.close()
    _emit(nc, p, sems)
    ctx.close()
    return nc




def build_l1(dbg=0):
    nc = bass.Bass("TRN2", target_bir_lowering=False, debug=False)
    p = Prog()
    ap = {}

    def din(name, shape):
        ap[name] = nc.dram_tensor(name, list(shape), F32, kind="ExternalInput").ap()

    din("x1r", (S, C))
    din("x1q", (NT_OWN * 128, C))
    din("l1qT", (C, C)); din("l1kT", (C, C)); din("l1vT", (C, C)); din("l1oT", (C, C))
    din("bq16", (16, 64)); din("bk16", (16, 64)); din("bv_", (1, C)); din("bo_", (1, C))
    din("wsT", (C, 16)); din("bs16", (16, 1))
    din("cw1", (C, C)); din("cw2", (C, 512)); din("cw3", (512, 256)); din("cw4", (256, 1))
    din("cb1r", (8, 128)); din("cb2r", (4, 128)); din("cb3r", (2, 128)); din("cb4_", (1, 1))
    din("gw1", (C, 512)); din("gw2", (512, 256)); din("gw3", (256, 1))
    din("gb1_", (1, 512)); din("gb2_", (1, 256)); din("gb3_", (1, 1)); din("ar_", (1, 1))
    din("lmask", (2, 128, 256))
    din("idm", (128, 128))
    y1 = nc.dram_tensor("y1", [NT_OWN * 128, C], F32, kind="ExternalOutput").ap()
    att1_d = nc.dram_tensor("att1_d", [NT_OWN * 128, C], F32,
                            kind=("ExternalOutput" if dbg else "Internal")).ap()
    if dbg:
        ds_ = nc.dram_tensor("ds_", [16, S], F32, kind="ExternalOutput").ap()
        dbis = nc.dram_tensor("dbis", [16, 8], F32, kind="ExternalOutput").ap()
        dselb = nc.dram_tensor("dselb", [128, 512], F32, kind="ExternalOutput").ap()
        dgp = nc.dram_tensor("dgp", [1, 8], F32, kind="ExternalOutput").ap()
        dq1 = nc.dram_tensor("dq1", [4, 64, PHYSW], F32, kind="ExternalOutput").ap()
        dk1 = nc.dram_tensor("dk1", [4, 64, S], F32, kind="ExternalOutput").ap()
        dv1 = nc.dram_tensor("dv1", [NT_ABS, 128, 264], F32, kind="ExternalOutput").ap()
        dab = nc.dram_tensor("dab", [4, 64, 2048], F32, kind="ExternalOutput").ap()
        dffn = nc.dram_tensor("dffn", [NT_OWN * 128, C], F32, kind="ExternalOutput").ap()
        dloc = nc.dram_tensor("dloc", [64, 2048], F32, kind="ExternalOutput").ap()
        dtf = nc.dram_tensor("dtf", [6, 512], F32, kind="ExternalOutput").ap()
        dc0 = nc.dram_tensor("dc0", [NT_OWN * 128, C], F32, kind="ExternalOutput").ap()
        dtop0 = nc.dram_tensor("dtop0", [64, 2048], F32, kind="ExternalOutput").ap()
    fmA = nc.dram_tensor("fmA", [8, 128, S], BF16, kind="Internal").ap()
    fmQ = nc.dram_tensor("fmQ", [8, 128, NT_OWN * 128], BF16, kind="Internal").ap()

    ctx = ExitStack()
    sems = {e: ctx.enter_context(nc.semaphore("sem_" + e)) for e in ENGS}

    def sball(st, name, shape, dt):
        return st.enter_context(nc.sbuf_tensor(name, list(shape), dt))

    def psall(st, name, shape, dt=F32):
        return st.enter_context(nc.psum_tensor(name, list(shape), dt))

    ident = sball(ctx, "ident", (128, 128), BF16)
    identf = sball(ctx, "identf", (128, 128), F32)
    ones2 = sball(ctx, "ones2", (65, 128), F32)
    third = sball(ctx, "third", (65, 128), F32)
    onesb = sball(ctx, "onesb", (128, 1), BF16)
    consts = sball(ctx, "consts", (128, 8), F32)
    nc.const_aps.register(F32, 0.125, consts[:, 2:3])
    nc.const_aps.register(F32, -0.5, consts[:, 3:4])
    nc.const_aps.register(F32, 1.0 / 1024.0, consts[:, 4:5])
    nc.const_aps.register(F32, 1.0, consts[:, 5:6])
    C20 = consts[:, 0:1]
    CEPS = consts[:, 1:2]

    gpar = sball(ctx, "gpar", (1, 8), F32)
    ps_tr = psall(ctx, "ps_tr", (128, 1024), BF16)
    ps_a = [psall(ctx, "ps_a0", (128, 512)), psall(ctx, "ps_a1", (128, 512))]
    ps_b = [psall(ctx, "ps_b0", (128, 512)), psall(ctx, "ps_b1", (128, 512))]
    ps_c = [psall(ctx, "ps_c0", (128, 512)), psall(ctx, "ps_c1", (128, 512))]
    ps_d = psall(ctx, "ps_d", (128, 512))

    bank_free = {}
    war = {}

    def use(name, tok):
        war.setdefault(name, []).append(tok)

    def takew(name):
        return war.pop(name, [])

    def pop_bank(nm):
        return [bank_free.pop(nm)] if nm in bank_free else []

    tk_id = p.g(lambda e: e.dma_start(out=ident[:], in_=ap["idm"][:]))
    tk_idf = p.y(lambda e: e.dma_start(out=identf[:], in_=ap["idm"][:]))
    tk = p.v(lambda e: e.memset(consts[:], 0.0))
    tk = p.v(lambda e: e.memset(consts[:, 0:1], -CSHIFT), after=[tk])
    tk = p.v(lambda e: e.memset(consts[:, 1:2], EPS_LN), after=[tk])
    tk = p.v(lambda e: e.memset(consts[:, 2:3], 0.125), after=[tk])
    tk = p.v(lambda e: e.memset(consts[:, 3:4], -0.5), after=[tk])
    tk = p.v(lambda e: e.memset(consts[:, 4:5], 1.0 / 1024.0), after=[tk])
    tk = p.v(lambda e: e.memset(consts[:, 5:6], 1.0), after=[tk])
    tk_consts = tk
    tk_ones = p.v(lambda e: e.memset(ones2[:], 1.0), after=[tk_consts])
    tk_third = p.v(lambda e: e.memset(third[:], 1.0 / 3.0), after=[tk_ones])
    tk_onesb = p.v(lambda e: e.memset(onesb[:], 1.0), after=[tk_third])

    def ln_chain(xsrc_tok, xbuf, xnbuf, stbuf, extra_after=()):
        t1 = p.v(lambda e: e.reduce_sum(stbuf[:, 0:1], xbuf[:], axis=AX.X),
                 after=[xsrc_tok, tk_consts, *extra_after])
        t2 = p.s(lambda e: e.activation(out=xnbuf[:], in_=xbuf[:], func=AF.Square,
                                        accum_out=stbuf[:, 2:3]),
                 after=[xsrc_tok, tk_consts, *extra_after])
        t3 = p.v(lambda e: e.tensor_scalar_mul(stbuf[:, 1:2], stbuf[:, 0:1], 1.0 / 1024.0), after=[t1])
        t4 = p.v(lambda e: e.tensor_scalar_mul(stbuf[:, 3:4], stbuf[:, 2:3], 1.0 / 1024.0), after=[t2])
        t5 = p.v(lambda e: e.tensor_mul(stbuf[:, 4:5], stbuf[:, 1:2], stbuf[:, 1:2]), after=[t3])
        t5b = p.v(lambda e: e.tensor_sub(stbuf[:, 5:6], stbuf[:, 3:4], stbuf[:, 4:5]), after=[t4, t5])
        t6 = p.s(lambda e: e.activation(out=stbuf[:, 6:7], in_=stbuf[:, 5:6], func=AF.Ln,
                                        bias=CEPS), after=[t5b])
        t7 = p.s(lambda e: e.activation(out=stbuf[:, 7:8], in_=stbuf[:, 6:7], func=AF.Exp,
                                        scale=-0.5), after=[t6])
        t8a = p.v(lambda e: e.tensor_single_scalar(xnbuf[:], xbuf[:], stbuf[:, 1:2], ALU.subtract),
                  after=[t7, t2])
        t8 = p.v(lambda e: e.tensor_single_scalar(xnbuf[:], xnbuf[:], stbuf[:, 7:8], ALU.mult),
                 after=[t8a])
        return t8

    def w0_of(kt):
        return (kt + 1) // 2

    def abs_row(t):
        return (t % 2) * 2048 + (t // 2) * 128

    NHQ = 4
    NPASS = 4

    sA = ExitStack()
    selb = sball(sA, "selb", (128, 32 * 16), F32)
    lmask_sb = sball(sA, "lmask_sb", (128, 2 * 256), F32)
    att1h = sball(sA, "att1h", (64, 2048), F32)
    x1fm = sball(sA, "x1fm", (128, 8 * 512), BF16)
    linv_sb = sball(sA, "linv_sb", (128, 512), F32)
    linvB = sball(sA, "linvB", (128, 512), F32)
    tmp_sb = sball(sA, "tmp_sb", (128, 512), F32)
    pt_sb = [sball(sA, f"pt{i}", (128, 512), BF16) for i in range(2)]
    oout = [sball(sA, f"oout{i}", (128, 512), F32) for i in range(2)]
    bqT = sball(sA, "bqT", (64, 16), F32)
    bkT = sball(sA, "bkT", (64, 16), F32)

    tk_lm = p.y(lambda e: e.dma_start(out=lmask_sb[:].rearrange("p (k q) -> p k q", k=2),
                                      in_=ap["lmask"].rearrange("k p q -> p k q")))

    sP = ExitStack()
    xb_t = [sball(sP, f"xbt{i}", (128, C), F32) for i in range(2)]
    xn_t = [sball(sP, f"xnt{i}", (128, C), F32) for i in range(2)]
    xnb_t = [sball(sP, f"xnbt{i}", (128, C), BF16) for i in range(2)]
    st_t = [sball(sP, f"stt{i}", (128, 8), F32) for i in range(2)]
    s_fm = sball(sP, "s_fm", (16, S), F32)
    bscr = sball(sP, "bscr", (16, S), BF16)
    bis = sball(sP, "bis", (16, 8), F32)
    tauB = sball(sP, "tauB", (128, 16), F32)
    ws_sb = sball(sP, "ws_sb", (128, 8 * 16), BF16)
    bs_sb = sball(sP, "bs_sb", (16, 1), F32)
    gw1_sb = sball(sP, "gw1_sb", (128, 8 * 512), BF16)
    gw2_sb = sball(sP, "gw2_sb", (128, 4 * 256), BF16)
    gw3_sb = sball(sP, "gw3_sb", (128, 2 * 1), BF16)
    gb1_sb = sball(sP, "gb1_sb", (1, 512), F32)
    gb2_sb = sball(sP, "gb2_sb", (1, 256), F32)
    gb3_sb = sball(sP, "gb3_sb", (1, 1), F32)
    ar_sb = sball(sP, "ar_sb", (1, 1), F32)
    gfr = sball(sP, "gfr", (1, C), F32)
    gf_misc = sball(sP, "gfm", (128, 16), F32)
    gf_bf = sball(sP, "gfb", (128, 16), BF16)
    grow = sball(sP, "grow", (1, 512), F32)
    growb = sball(sP, "growb", (1, 512), BF16)
    bq_r = sball(sP, "bq_r", (16, 64), F32)

    lw_s = p.g(lambda e: e.dma_start(out=ws_sb[:].rearrange("p (c o) -> p c o", c=8),
                                     in_=ap["wsT"].rearrange("(c p) o -> p c o", p=128)))
    t_bs = p.y(lambda e: e.dma_start(out=bs_sb[:], in_=ap["bs16"][:]))
    t_gw1 = p.g(lambda e: e.dma_start(out=gw1_sb[:].rearrange("p (c o) -> p c o", c=8),
                                      in_=ap["gw1"].rearrange("(c p) o -> p c o", p=128)))
    t_gw2 = p.g(lambda e: e.dma_start(out=gw2_sb[:].rearrange("p (c o) -> p c o", c=4),
                                      in_=ap["gw2"].rearrange("(c p) o -> p c o", p=128)))
    t_gw3 = p.g(lambda e: e.dma_start(out=gw3_sb[:].rearrange("p (c o) -> p c o", c=2),
                                      in_=ap["gw3"].rearrange("(c p) o -> p c o", p=128),
                                      allow_slow_non_contiguous=True))
    t_gb1 = p.y(lambda e: e.dma_start(out=gb1_sb[:], in_=ap["gb1_"][:]))
    t_gb2 = p.y(lambda e: e.dma_start(out=gb2_sb[:], in_=ap["gb2_"][:]))
    t_gb3 = p.y(lambda e: e.dma_start(out=gb3_sb[:], in_=ap["gb3_"][:]))
    t_ar = p.y(lambda e: e.dma_start(out=ar_sb[:], in_=ap["ar_"][:]))
    t_bq = p.y(lambda e: e.dma_start(out=bq_r[:], in_=ap["bq16"][:]))
    tq1 = p.t(lambda e: e.transpose(ps_d[0:64, 0:16], bq_r[:], identf[0:16, 0:16]),
              after=[t_bq, tk_idf] + pop_bank("d"))
    cq1 = p.v(lambda e: e.tensor_copy(bqT[:], ps_d[0:64, 0:16]), after=[tq1])
    t_bk = p.y(lambda e: e.dma_start(out=bq_r[:], in_=ap["bk16"][:]), after=[tq1])
    tq2 = p.t(lambda e: e.transpose(ps_d[0:64, 16:32], bq_r[:], identf[0:16, 0:16]), after=[t_bk, cq1])
    cq2 = p.v(lambda e: e.tensor_copy(bkT[:], ps_d[0:64, 16:32]), after=[tq2])
    bank_free["d"] = cq2

    def p0_stream(n_tiles, src_of, fm_out, do_sg):
        for t in range(n_tiles):
            i = t % 2
            ld = p.y(lambda e, t=t, i=i: e.dma_start(out=xb_t[i][:], in_=src_of(t)),
                     after=takew(f"xb{i}"))
            xn_tok = ln_chain(ld, xb_t[i], xn_t[i], st_t[i], extra_after=takew(f"ln{i}"))
            use(f"xb{i}", xn_tok)
            use(f"ln{i}", xn_tok)
            hi = p.v(lambda e, i=i: e.tensor_copy(xnb_t[i][:], xn_t[i][:]),
                     after=[xn_tok] + takew(f"xnb{i}"))
            use(f"ln{i}", hi)
            if do_sg:
                for ch in range(2):
                    gfm = p.t(lambda e, t=t, ch=ch, i=i: e.matmul(
                        ps_c[ch][0:1, 0:512], onesb[:], xnb_t[i][:, ch * 512:(ch + 1) * 512],
                        start=(t == 0), stop=(t == n_tiles - 1)),
                        after=([hi, tk_onesb] + (pop_bank(f"c{ch}") if t == 0 else [])))
                    use(f"xnb{i}", gfm)
            trs = None
            for cc in range(8):
                trs = p.t(lambda e, cc=cc, i=i: e.transpose(
                    ps_tr[:, cc * 128:(cc + 1) * 128], xnb_t[i][:, cc * 128:(cc + 1) * 128], ident[:]),
                    after=([hi, tk_id] + pop_bank("tr") if cc == 0 else []), inc=(1 if cc == 7 else 0))
            use(f"xnb{i}", trs)
            cpy = None
            for cc in range(8):
                cpy = p.v(lambda e, cc=cc, t=t: e.tensor_copy(
                    x1fm[:, cc * 512 + (t % 4) * 128: cc * 512 + (t % 4) * 128 + 128],
                    ps_tr[:, cc * 128:(cc + 1) * 128]),
                    after=[trs] + (takew(f"xf{t % 4}") if cc == 0 else []))
            bank_free["tr"] = cpy
            if t % 4 == 3:
                g0 = t - 3
                stw = p.g(lambda e, g0=g0, fm_out=fm_out: e.dma_start(
                    out=fm_out.rearrange("c p w -> p c w")[:, :, g0 * 128:(g0 + 4) * 128],
                    in_=x1fm[:].rearrange("p (c w) -> p c w", c=8)), after=[cpy])
                for sl in range(4):
                    use(f"xf{sl}", stw)
                if do_sg:
                    saft = [cpy, lw_s, t_bs] + pop_bank("d")
                    smm = None
                    for cc in range(8):
                        smm = p.t(lambda e, cc=cc: e.matmul(
                            ps_d[0:16, 0:512], ws_sb[:, cc * 16:(cc + 1) * 16],
                            x1fm[:, cc * 512:(cc + 1) * 512], start=(cc == 0), stop=(cc == 7)),
                            after=(saft if cc == 0 else []), inc=(1 if cc == 7 else 0))
                    se = p.s(lambda e, g0=g0: e.activation(
                        out=s_fm[:, g0 * 128:(g0 + 4) * 128], in_=ps_d[0:16, 0:512], func=AF.Identity,
                        bias=bs_sb[:]), after=[smm])
                    bank_free["d"] = se
                    use("sfm_w", se)
                    for sl in range(4):
                        use(f"xf{sl}", smm)

    p0_stream(NT_ABS, lambda t: ap["x1r"][abs_row(t):abs_row(t) + 128, :], fmA, True)
    gf_mm_done = p.now("t")
    p0_stream(NT_OWN, lambda t: ap["x1q"][t * 128:(t + 1) * 128, :], fmQ, False)

    gfc0 = p.v(lambda e: e.tensor_copy(gfr[0:1, 0:512], ps_c[0][0:1, 0:512]), after=[gf_mm_done])
    gfc1 = p.v(lambda e: e.tensor_copy(gfr[0:1, 512:1024], ps_c[1][0:1, 0:512]), after=[gfc0])
    bank_free["c0"] = gfc1
    bank_free["c1"] = gfc1
    gfs = p.v(lambda e: e.tensor_scalar_mul(gfr[:], gfr[:], 1.0 / 4096.0), after=[gfc1])
    gtok = gfs
    for cc in range(8):
        tt = p.t(lambda e, cc=cc: e.transpose(ps_d[0:128, 0:1], gfr[0:1, cc * 128:(cc + 1) * 128],
                                              identf[0:1, 0:1]), after=[gtok] + pop_bank("d"))
        gtok = p.v(lambda e, cc=cc: e.tensor_copy(gf_misc[:, cc:cc + 1], ps_d[0:128, 0:1]),
                   after=[tt])
        bank_free["d"] = gtok
    gcb = p.v(lambda e: e.tensor_copy(gf_bf[:, 0:8], gf_misc[:, 0:8]), after=[gtok])
    g1m = None
    for cc in range(8):
        g1m = p.t(lambda e, cc=cc: e.matmul(
            ps_d[0:1, 0:512], gf_bf[:, cc:cc + 1], gw1_sb[:, cc * 512:(cc + 1) * 512],
            start=(cc == 0), stop=(cc == 7)),
            after=([gcb, t_gw1] + pop_bank("d") if cc == 0 else []), inc=(1 if cc == 7 else 0))
    ga1 = p.v(lambda e: e.tensor_add(grow[:], ps_d[0:1, 0:512], gb1_sb[:]), after=[g1m, t_gb1])
    bank_free["d"] = ga1
    gg1 = p.s(lambda e: e.activation(out=grow[:], in_=grow[:], func=AF.Gelu), after=[ga1])
    gc1_ = p.v(lambda e: e.tensor_copy(growb[:], grow[:]), after=[gg1])
    gtok = gc1_
    for cc in range(4):
        tt = p.t(lambda e, cc=cc: e.transpose(ps_tr[:, 0:1], growb[0:1, cc * 128:(cc + 1) * 128],
                                              ident[0:1, 0:1]), after=[gtok] + pop_bank("tr"))
        gtok = p.v(lambda e, cc=cc: e.tensor_copy(gf_bf[:, 8 + cc:9 + cc], ps_tr[:, 0:1]),
                   after=[tt])
        bank_free["tr"] = gtok
    g2m = None
    for cc in range(4):
        g2m = p.t(lambda e, cc=cc: e.matmul(
            ps_d[0:1, 0:256], gf_bf[:, 8 + cc:9 + cc], gw2_sb[:, cc * 256:(cc + 1) * 256],
            start=(cc == 0), stop=(cc == 3)),
            after=([gtok, t_gw2] + pop_bank("d") if cc == 0 else []), inc=(1 if cc == 3 else 0))
    ga2 = p.v(lambda e: e.tensor_add(grow[0:1, 0:256], ps_d[0:1, 0:256], gb2_sb[:]), after=[g2m, t_gb2])
    bank_free["d"] = ga2
    gg2 = p.s(lambda e: e.activation(out=grow[0:1, 0:256], in_=grow[0:1, 0:256], func=AF.Gelu), after=[ga2])
    gc2_ = p.v(lambda e: e.tensor_copy(growb[0:1, 0:256], grow[0:1, 0:256]), after=[gg2])
    gtok = gc2_
    for cc in range(2):
        tt = p.t(lambda e, cc=cc: e.transpose(ps_tr[:, 0:1], growb[0:1, cc * 128:(cc + 1) * 128],
                                              ident[0:1, 0:1]), after=[gtok] + pop_bank("tr"))
        gtok = p.v(lambda e, cc=cc: e.tensor_copy(gf_bf[:, 12 + cc:13 + cc], ps_tr[:, 0:1]),
                   after=[tt])
        bank_free["tr"] = gtok
    g3m = None
    for cc in range(2):
        g3m = p.t(lambda e, cc=cc: e.matmul(
            ps_d[0:1, 0:1], gf_bf[:, 12 + cc:13 + cc], gw3_sb[:, cc:cc + 1],
            start=(cc == 0), stop=(cc == 1)),
            after=([gtok, t_gw3] + pop_bank("d") if cc == 0 else []), inc=(1 if cc == 1 else 0))
    ga3 = p.v(lambda e: e.tensor_add(gpar[0:1, 0:1], ps_d[0:1, 0:1], gb3_sb[:]), after=[g3m, t_gb3])
    bank_free["d"] = ga3
    gg3 = p.s(lambda e: e.activation(out=gpar[0:1, 0:1], in_=gpar[0:1, 0:1], func=AF.Sigmoid), after=[ga3])
    g7a = p.v(lambda e: e.tensor_scalar_mul(gpar[0:1, 1:2], gpar[0:1, 0:1], 0.7), after=[gg3])
    g7b = p.v(lambda e: e.tensor_scalar_add(gpar[0:1, 1:2], gpar[0:1, 1:2], -0.5), after=[g7a])
    gf_done = p.v(lambda e: e.tensor_scalar_mul(gpar[0:1, 2:3], ar_sb[:], 2.0), after=[g7b, t_ar])

    sw = takew("sfm_w")
    b1 = p.v(lambda e: e.reduce_max(bis[:, 1:2], s_fm[:], axis=AX.X), after=sw)
    btok = p.v(lambda e: e.tensor_reduce(bis[:, 0:1], s_fm[:], AX.X, ALU.min), after=[b1])
    for it in range(28):
        m1 = p.v(lambda e: e.tensor_add(bis[:, 2:3], bis[:, 0:1], bis[:, 1:2]), after=[btok])
        m2 = p.v(lambda e: e.tensor_scalar_mul(bis[:, 2:3], bis[:, 2:3], 0.5), after=[m1])
        m3 = p.v(lambda e: e.tensor_scalar(bscr[:], s_fm[:], bis[:, 2:3], 0.0, ALU.is_ge,
                                           ALU.add, accum_out=bis[:, 3:4]), after=[m2])
        m4 = p.v(lambda e: e.tensor_single_scalar(bis[:, 4:5], bis[:, 3:4], 1024.0, ALU.is_ge),
                 after=[m3])
        m5 = p.v(lambda e: e.tensor_sub(bis[:, 5:6], bis[:, 2:3], bis[:, 0:1]), after=[m4])
        m6 = p.v(lambda e: e.tensor_mul(bis[:, 5:6], bis[:, 5:6], bis[:, 4:5]), after=[m5])
        m7 = p.v(lambda e: e.tensor_add(bis[:, 0:1], bis[:, 0:1], bis[:, 5:6]), after=[m6])
        m8 = p.v(lambda e: e.tensor_sub(bis[:, 6:7], bis[:, 1:2], bis[:, 2:3]), after=[m7])
        m9 = p.v(lambda e: e.tensor_mul(bis[:, 6:7], bis[:, 6:7], bis[:, 4:5]), after=[m8])
        btok = p.v(lambda e: e.tensor_add(bis[:, 1:2], bis[:, 2:3], bis[:, 6:7]), after=[m9])
    tt = p.t(lambda e: e.transpose(ps_d[0:1, 0:16], bis[:, 0:1], identf[0:16, 0:16]),
             after=[btok, tk_idf] + pop_bank("d"))
    tc = p.v(lambda e: e.tensor_copy(gf_misc[0:1, 0:16], ps_d[0:1, 0:16]), after=[tt])
    tb = p.t(lambda e: e.matmul(ps_d[:, 0:16], ones2[0:1, :], gf_misc[0:1, 0:16],
                                start=True, stop=True), after=[tc])
    tbc = p.v(lambda e: e.tensor_copy(tauB[:], ps_d[:, 0:16]), after=[tb])
    bank_free["d"] = tbc
    seltok = None
    for kt in range(NT_ABS):
        ts1 = p.t(lambda e, kt=kt: e.transpose(
            ps_d[0:128, 16:32], s_fm[:, kt * 128:(kt + 1) * 128], identf[0:16, 0:16]),
            after=[tbc] + ([seltok] if seltok else []))
        seltok = p.v(lambda e, kt=kt: e.tensor_tensor(
            selb[:, kt * 16:(kt + 1) * 16], ps_d[0:128, 16:32], tauB[:], ALU.is_ge),
            after=[ts1])
    sb1 = p.v(lambda e: e.tensor_scalar_mul(selb[:], selb[:], 0.125e9), after=[seltok])
    sel_done = p.v(lambda e: e.tensor_scalar_add(selb[:], selb[:], -0.125e9 - CSHIFT), after=[sb1])

    if dbg:
        dd1 = p.y(lambda e: e.dma_start(out=ds_[:], in_=s_fm[:]), after=p.snap())
        dd2 = p.y(lambda e: e.dma_start(out=dbis[:], in_=bis[:]), after=[dd1])
        dd3 = p.y(lambda e: e.dma_start(out=dselb[:], in_=selb[:]), after=[dd2])
        dd4 = p.y(lambda e: e.dma_start(out=dgp[:], in_=gpar[:]), after=[dd3])
        use("sfm_r", dd4)
    p0_done = p.snap()
    sP.close()

    sH = ExitStack()
    q1 = [sball(sH, f"q1_{hh}", (64, PHYSW), BF16) for hh in range(NHQ)]
    k1 = [sball(sH, f"k1_{hh}", (64, S), BF16) for hh in range(NHQ)]
    v1 = [sball(sH, f"v1_{t}", (128, NHQ * 66), BF16) for t in range(NT_ABS)]
    att_bf = [sball(sH, f"abf{hh}", (64, 2048), BF16) for hh in range(NHQ)]
    wq_sb = sball(sH, "wq_sb", (128, 8 * 256), BF16)
    wk_sb = sball(sH, "wk_sb", (128, 8 * 256), BF16)
    wv_sb = sball(sH, "wv_sb", (128, 8 * 256), BF16)
    wo_h = [sball(sH, f"wo{hh}", (64, C), BF16) for hh in range(NHQ)]
    bvB = sball(sH, "bvB", (128, 256), F32)
    bv_row = sball(sH, "bv_row", (1, 256), F32)

    for hp in range(NPASS):
        h0 = hp * NHQ
        lw_q = p.g(lambda e, h0=h0: e.dma_start(
            out=wq_sb[:].rearrange("p (c o) -> p c o", c=8),
            in_=ap["l1qT"].rearrange("(c p) o -> p c o", p=128)[:, :, h0 * 64:(h0 + NHQ) * 64]),
            after=takew("wq") + (p0_done if hp == 0 else []))
        lw_k = p.g(lambda e, h0=h0: e.dma_start(
            out=wk_sb[:].rearrange("p (c o) -> p c o", c=8),
            in_=ap["l1kT"].rearrange("(c p) o -> p c o", p=128)[:, :, h0 * 64:(h0 + NHQ) * 64]),
            after=takew("wk"))
        lw_v = p.g(lambda e, h0=h0: e.dma_start(
            out=wv_sb[:].rearrange("p (c o) -> p c o", c=8),
            in_=ap["l1vT"].rearrange("(c p) o -> p c o", p=128)[:, :, h0 * 64:(h0 + NHQ) * 64]),
            after=takew("wv"))
        lw_o = {}
        for hh in range(NHQ):
            lw_o[hh] = p.g(lambda e, hh=hh, h0=h0: e.dma_start(
                out=wo_h[hh][:], in_=ap["l1oT"][(h0 + hh) * 64:(h0 + hh) * 64 + 64, :]),
                after=takew(f"wo{hh}"))
        lb_v = p.y(lambda e, h0=h0: e.dma_start(out=bv_row[:], in_=ap["bv_"][:, h0 * 64:(h0 + NHQ) * 64]),
                   after=takew("bvrow") + (p0_done if hp == 0 else []))
        bvm = p.t(lambda e: e.matmul(ps_d[:, 0:256], ones2[0:1, :], bv_row[:], start=True, stop=True),
                  after=[lb_v, tk_ones] + pop_bank("d"))
        use("bvrow", bvm)
        bvc = p.v(lambda e: e.tensor_copy(bvB[:], ps_d[:, 0:256]), after=[bvm] + takew("bvB"))
        bank_free["d"] = bvc

        for t4 in range(NT_ABS // 4):
            g0 = t4 * 4
            lfm = p.g(lambda e, g0=g0: e.dma_start(
                out=x1fm[:].rearrange("p (c w) -> p c w", c=8),
                in_=fmA.rearrange("c p w -> p c w")[:, :, g0 * 128:(g0 + 4) * 128]), after=takew("xfm"))
            for sl in range(4):
                t = g0 + sl
                pv = ps_a[t % 2]
                vaft = [lfm, lw_v] + pop_bank(f"a{t % 2}")
                vmm = None
                for cc in range(8):
                    vmm = p.t(lambda e, cc=cc, sl=sl, pv=pv: e.matmul(
                        pv[:, 0:256], x1fm[:, cc * 512 + sl * 128: cc * 512 + sl * 128 + 128],
                        wv_sb[:, cc * 256:(cc + 1) * 256], start=(cc == 0), stop=(cc == 7)),
                        after=(vaft if cc == 0 else []), inc=(1 if cc == 7 else 0))
                use("xfm", vmm)
                use("wv", vmm)
                vz = p.v(lambda e, t=t: e.memset(v1[t][:], 0.0), after=takew(f"v1_{t}"))
                vo = p.v(lambda e, t=t: e.memset(
                    v1[t][:].rearrange("p (g w) -> p g w", w=66)[:, :, 64:65], 1.0), after=[vz])
                ve = p.v(lambda e, t=t, pv=pv: e.tensor_tensor(
                    v1[t][:].rearrange("p (g w) -> p g w", w=66)[:, :, 0:64],
                    pv[:, 0:256].rearrange("p (g w) -> p g w", w=64),
                    bvB[:, 0:256].rearrange("p (g w) -> p g w", w=64), ALU.add),
                    after=[vmm, vo, bvc])
                bank_free[f"a{t % 2}"] = ve
                use("bvB", ve)
            for hh in range(NHQ):
                pk = ps_b[hh % 2]
                kaft = [lfm, lw_k, cq2] + pop_bank(f"b{hh % 2}")
                kmm = None
                for cc in range(8):
                    kmm = p.t(lambda e, cc=cc, hh=hh, pk=pk: e.matmul(
                        pk[0:64, 0:512], wk_sb[:, cc * 256 + hh * 64: cc * 256 + hh * 64 + 64],
                        x1fm[:, cc * 512:(cc + 1) * 512], start=(cc == 0), stop=(cc == 7)),
                        after=(kaft if cc == 0 else []), inc=(1 if cc == 7 else 0))
                use("xfm", kmm)
                use("wk", kmm)
                ke = p.s(lambda e, hh=hh, h0=h0, g0=g0, pk=pk: e.activation(
                    out=k1[hh][:, g0 * 128:(g0 + 4) * 128], in_=pk[0:64, 0:512], func=AF.Identity,
                    bias=bkT[:, h0 + hh: h0 + hh + 1]),
                    after=[kmm] + (takew(f"k1_{hh}") if g0 == 0 else []))
                bank_free[f"b{hh % 2}"] = ke
        for t4 in range(NT_OWN // 4):
            g0 = t4 * 4
            lfm = p.g(lambda e, g0=g0: e.dma_start(
                out=x1fm[:].rearrange("p (c w) -> p c w", c=8),
                in_=fmQ.rearrange("c p w -> p c w")[:, :, g0 * 128:(g0 + 4) * 128]), after=takew("xfm"))
            for hh in range(NHQ):
                pq = ps_b[hh % 2]
                qaft = [lfm, lw_q, cq1] + pop_bank(f"b{hh % 2}")
                qmm = None
                for cc in range(8):
                    qmm = p.t(lambda e, cc=cc, hh=hh, pq=pq: e.matmul(
                        pq[0:64, 0:512], wq_sb[:, cc * 256 + hh * 64: cc * 256 + hh * 64 + 64],
                        x1fm[:, cc * 512:(cc + 1) * 512], start=(cc == 0), stop=(cc == 7)),
                        after=(qaft if cc == 0 else []), inc=(1 if cc == 7 else 0))
                use("xfm", qmm)
                use("wq", qmm)
                qe = p.s(lambda e, hh=hh, h0=h0, t4=t4, pq=pq: e.activation(
                    out=q1[hh][:, 128 + t4 * 512: 128 + (t4 + 1) * 512], in_=pq[0:64, 0:512],
                    func=AF.Identity, bias=bqT[:, h0 + hh: h0 + hh + 1]),
                    after=[qmm] + (takew(f"q1_{hh}") if t4 == 0 else []))
                bank_free[f"b{hh % 2}"] = qe
        qpads = []
        for hh in range(NHQ):
            pz1 = p.v(lambda e, hh=hh: e.memset(q1[hh][:, 0:128], 0.0), after=takew(f"q1p{hh}"))
            pz2 = p.v(lambda e, hh=hh: e.memset(q1[hh][:, 2176:PHYSW], 0.0), after=[pz1])
            qpads.append(pz2)
        prep_done = p.snap()

        for hh in range(NHQ):
            hg = h0 + hh
            a1h_tok = None
            for pp in range(1, 17):
                kts = [kt for kt in range(NT_ABS) if w0_of(kt) in (pp - 1, pp)]
                reg = ps_b[pp % 2]
                evac_dep = pop_bank(f"b{pp % 2}")
                first = True
                pvt = None
                for kt in kts:
                    pos = pp - w0_of(kt)
                    pa = ps_a[kt % 2]
                    saft = prep_done + [tk_lm, qpads[hh]] + pop_bank(f"a{kt % 2}")
                    smm = p.t(lambda e, hh=hh, kt=kt, pp=pp, pa=pa: e.matmul(
                        pa[0:128, 0:128], k1[hh][:, kt * 128:(kt + 1) * 128],
                        q1[hh][:, pp * 128:(pp + 1) * 128], start=True, stop=True),
                        after=saft)
                    msk = p.v(lambda e, kt=kt, pos=pos, pa=pa: e.tensor_add(
                        pa[0:128, 0:128], pa[0:128, 0:128],
                        lmask_sb[:, (kt % 2) * 256 + pos * 128: (kt % 2) * 256 + pos * 128 + 128]),
                        after=[smm, tk_lm])
                    ex = p.s(lambda e, kt=kt, pa=pa: e.activation(
                        out=pt_sb[kt % 2][:, 0:128], in_=pa[0:128, 0:128], func=AF.Exp,
                        scale=0.125, bias=C20), after=[msk] + takew(f"pt{kt % 2}"))
                    bank_free[f"a{kt % 2}"] = ex
                    pvt = p.t(lambda e, kt=kt, hh=hh, first=first, last=(kt == kts[-1]), reg=reg: e.matmul(
                        reg[0:65, 0:128], v1[kt][:, hh * 66: hh * 66 + 65],
                        pt_sb[kt % 2][:, 0:128], start=first, stop=last),
                        after=[ex] + (evac_dep if first else []), inc=(1 if kt == kts[-1] else 0))
                    use(f"pt{kt % 2}", pvt)
                    use(f"v1_{kt}", pvt)
                    first = False
                rc1 = p.v(lambda e, reg=reg: e.reciprocal(linv_sb[64:65, 0:128], reg[64:65, 0:128]),
                          after=[pvt] + takew("linv"))
                bc = p.t(lambda e: e.matmul(ps_d[:, 0:128], third[64:65, :], linv_sb[64:65, 0:128],
                                            start=True, stop=True), after=[rc1, tk_third] + pop_bank("d"))
                use("linv", bc)
                lb = p.s(lambda e: e.activation(out=linvB[:, 0:128], in_=ps_d[:, 0:128], func=AF.Copy),
                         after=[bc] + takew("linvB"))
                bank_free["d"] = lb
                ev = p.v(lambda e, pp=pp, reg=reg: e.tensor_mul(
                    att1h[:, (pp - 1) * 128: pp * 128], reg[0:64, 0:128], linvB[0:64, 0:128]),
                    after=[lb] + ([a1h_tok] if a1h_tok else []) + takew("a1h"))
                use("linvB", ev)
                bank_free[f"b{pp % 2}"] = ev
                a1h_tok = ev
            if dbg and hp == 0 and hh == 0:
                dl_ = p.y(lambda e: e.dma_start(out=dloc[:], in_=att1h[:]), after=[a1h_tok])
                use("a1h", dl_)
                a1h_tok = a1h_tok
            for c4 in range(4):
                for br in range(2):
                    reg = ps_c[(c4 + 1) % 2]
                    evac_dep = pop_bank(f"c{(c4 + 1) % 2}")
                    if br == 0:
                        pvt = None
                        first = True
                        for kt in range(NT_ABS):
                            pa = ps_a[kt % 2]
                            smm = p.t(lambda e, hh=hh, kt=kt, c4=c4, pa=pa: e.matmul(
                                pa[0:128, 0:512], k1[hh][:, kt * 128:(kt + 1) * 128],
                                q1[hh][:, 128 + c4 * 512: 128 + (c4 + 1) * 512], start=True, stop=True),
                                after=[sel_done] + pop_bank(f"a{kt % 2}"))
                            ex = p.s(lambda e, kt=kt, hg=hg, pa=pa: e.activation(
                                out=pt_sb[kt % 2][:, 0:512], in_=pa[0:128, 0:512], func=AF.Exp,
                                scale=0.125, bias=selb[:, kt * 16 + hg: kt * 16 + hg + 1]),
                                after=[smm] + takew(f"pt{kt % 2}"))
                            bank_free[f"a{kt % 2}"] = ex
                            pvt = p.t(lambda e, kt=kt, hh=hh, first=first, last=(kt == NT_ABS - 1), reg=reg:
                                      e.matmul(reg[0:65, 0:512], v1[kt][:, hh * 66: hh * 66 + 65],
                                               pt_sb[kt % 2][:, 0:512], start=first, stop=last),
                                      after=[ex] + (evac_dep if first else []),
                                      inc=(1 if kt == NT_ABS - 1 else 0))
                            use(f"pt{kt % 2}", pvt)
                            use(f"v1_{kt}", pvt)
                            first = False
                    else:
                        gp = (c4 + 1) % 2
                        pa = ps_a[gp]
                        smm = p.t(lambda e, hh=hh, c4=c4, pa=pa: e.matmul(
                            pa[0:64, 0:512], k1[hh][:, 0:64],
                            q1[hh][:, 128 + c4 * 512: 128 + (c4 + 1) * 512], start=True, stop=True),
                            after=pop_bank(f"a{gp}"))
                        ex = p.s(lambda e, gp=gp, pa=pa: e.activation(
                            out=pt_sb[gp][0:64, 0:512], in_=pa[0:64, 0:512], func=AF.Exp,
                            scale=0.125, bias=consts[0:64, 0:1]), after=[smm] + takew(f"pt{gp}"))
                        bank_free[f"a{gp}"] = ex
                        pvt = p.t(lambda e, hh=hh, gp=gp, reg=reg: e.matmul(
                            reg[0:65, 0:512], v1[0][0:64, hh * 66: hh * 66 + 65],
                            pt_sb[gp][0:64, 0:512], start=True, stop=True),
                            after=[ex] + evac_dep)
                        use(f"pt{gp}", pvt)
                        use("v1_0", pvt)
                    rc1 = p.v(lambda e, reg=reg: e.reciprocal(linv_sb[64:65, 0:512], reg[64:65, 0:512]),
                              after=[pvt] + takew("linv"))
                    bc = p.t(lambda e: e.matmul(ps_d[:, 0:512], third[64:65, :], linv_sb[64:65, 0:512],
                                                start=True, stop=True), after=[rc1] + pop_bank("d"))
                    use("linv", bc)
                    lb = p.s(lambda e: e.activation(out=linvB[:, 0:512], in_=ps_d[:, 0:512], func=AF.Copy),
                             after=[bc] + takew("linvB"))
                    bank_free["d"] = lb
                    tm1 = p.v(lambda e, reg=reg: e.tensor_mul(tmp_sb[0:64, 0:512], reg[0:64, 0:512],
                                                              linvB[0:64, 0:512]),
                              after=[lb] + takew("tmp"))
                    use("linvB", tm1)
                    bank_free[f"c{(c4 + 1) % 2}"] = tm1
                    ev = p.v(lambda e, c4=c4: e.tensor_add(
                        att1h[:, c4 * 512:(c4 + 1) * 512], att1h[:, c4 * 512:(c4 + 1) * 512],
                        tmp_sb[0:64, 0:512]), after=[tm1, a1h_tok])
                    use("tmp", ev)
                    a1h_tok = ev
            if dbg and hp == 0 and hh == 0:
                dt_ = p.y(lambda e: e.dma_start(out=dtop0[:], in_=att1h[:]), after=[a1h_tok])
                use("a1h", dt_)
            cast = p.v(lambda e, hh=hh: e.tensor_copy(att_bf[hh][:], att1h[:]),
                       after=[a1h_tok] + takew(f"abf{hh}"))
            use("a1h", cast)
            use(f"q1p{hh}", cast)
        if dbg and hp == 0:
            dd = p.snap()
            for hh in range(NHQ):
                dd5 = p.g(lambda e, hh=hh: e.dma_start(out=dq1[hh], in_=q1[hh][:]), after=dd)
                dd5 = p.g(lambda e, hh=hh: e.dma_start(out=dk1[hh], in_=k1[hh][:]))
                dd5 = p.g(lambda e, hh=hh: e.dma_start(out=dab[hh], in_=att_bf[hh][:]))
                use(f"q1_{hh}", dd5); use(f"k1_{hh}", dd5); use(f"abf{hh}", dd5)
            for t in range(NT_ABS):
                dd6 = p.g(lambda e, t=t: e.dma_start(out=dv1[t][:, 0:264], in_=v1[t][:]))
                use(f"v1_{t}", dd6)
        ocast = p.now("v")
        for i2 in range(NT_OWN):
            for m2 in range(2):
                pc = ps_b[m2]
                oaft = [ocast] + [lw_o[hh] for hh in range(NHQ)] + pop_bank(f"b{m2}")
                omm = None
                for hh in range(NHQ):
                    omm = p.t(lambda e, hh=hh, m2=m2, i2=i2, pc=pc: e.matmul(
                        pc[:, 0:512], att_bf[hh][:, i2 * 128:(i2 + 1) * 128],
                        wo_h[hh][:, m2 * 512:(m2 + 1) * 512], start=(hh == 0), stop=(hh == NHQ - 1)),
                        after=(oaft if hh == 0 else []), inc=(1 if hh == NHQ - 1 else 0))
                for hh in range(NHQ):
                    use(f"abf{hh}", omm)
                    use(f"wo{hh}", omm)
                ocp = p.s(lambda e, m2=m2, pc=pc: e.activation(
                    out=oout[m2][:], in_=pc[:, 0:512], func=AF.Copy),
                    after=[omm] + takew(f"oout{m2}"))
                bank_free[f"b{m2}"] = ocp
                if hp == 0:
                    ost = p.y(lambda e, i2=i2, m2=m2: e.dma_start(
                        out=att1_d[i2 * 128:(i2 + 1) * 128, m2 * 512:(m2 + 1) * 512],
                        in_=oout[m2][:]), after=[ocp])
                else:
                    ost = p.g(lambda e, i2=i2, m2=m2: e.dma_start(
                        out=att1_d[i2 * 128:(i2 + 1) * 128, m2 * 512:(m2 + 1) * 512],
                        in_=oout[m2][:], accum_op=ALU.add), after=[ocp])
                use(f"oout{m2}", ost)

    attn_done = p.snap()
    sH.close()
    sA.close()

    sB = ExitStack()
    out_tm = [sball(sB, f"otm{i}", (128, C), F32) for i in range(NT_OWN)]
    out_fm = sball(sB, "out_fm", (128, 8 * 512), BF16)
    obf_t = [sball(sB, f"obf{i}", (128, C), BF16) for i in range(2)]
    fxb_t = [sball(sB, f"fxbt{i}", (128, C), F32) for i in range(2)]
    fst_t = [sball(sB, f"fstt{i}", (128, 8), F32) for i in range(2)]
    h1_fm = sball(sB, "h1_fm", (128, 8 * 512), BF16)
    h2_fm = sball(sB, "h2_fm", (128, 4 * 512), BF16)
    h3_fm = sball(sB, "h3_fm", (128, 2 * 512), BF16)
    cw1_sb = sball(sB, "cw1_sb", (128, 8 * C), BF16)
    cw2_sb = sball(sB, "cw2_sb", (128, 8 * 512), BF16)
    cw3_sb = sball(sB, "cw3_sb", (128, 4 * 256), BF16)
    cw4_sb = sball(sB, "cw4_sb", (128, 2), BF16)
    cb1T = sball(sB, "cb1T", (128, 8), F32)
    cb2T = sball(sB, "cb2T", (128, 4), F32)
    cb3T = sball(sB, "cb3T", (128, 2), F32)
    cb4_sb = sball(sB, "cb4_sb", (1, 1), F32)
    cbr = sball(sB, "cbr", (8, 128), F32)
    tfrow = sball(sB, "tfrow", (1, 512), F32)
    crow = sball(sB, "crow", (1, 512), F32)
    cnrow = sball(sB, "cnrow", (1, 512), F32)
    cT = sball(sB, "cT", (128, 2), F32)
    tq_sb = sball(sB, "tq_sb", (128, 256), F32)
    boB = sball(sB, "boB", (128, C), F32)
    bo_row = sball(sB, "bo_row", (1, C), F32)
    facc = [sball(sB, f"facc{i}", (128, C), F32) for i in range(2)]
    fatt = [sball(sB, f"fatt{i}", (128, C), F32) for i in range(2)]

    lcw1 = p.g(lambda e: e.dma_start(out=cw1_sb[:].rearrange("p (c o) -> p c o", c=8),
                                     in_=ap["cw1"].rearrange("(c p) o -> p c o", p=128)), after=attn_done)
    lcw2 = p.g(lambda e: e.dma_start(out=cw2_sb[:].rearrange("p (c o) -> p c o", c=8),
                                     in_=ap["cw2"].rearrange("(c p) o -> p c o", p=128)))
    lcw3 = p.g(lambda e: e.dma_start(out=cw3_sb[:].rearrange("p (c o) -> p c o", c=4),
                                     in_=ap["cw3"].rearrange("(c p) o -> p c o", p=128)))
    lcw4 = p.g(lambda e: e.dma_start(out=cw4_sb[:].rearrange("p (c o) -> p c o", c=2),
                                     in_=ap["cw4"].rearrange("(c p) o -> p c o", p=128),
                                     allow_slow_non_contiguous=True))
    lcb4 = p.y(lambda e: e.dma_start(out=cb4_sb[:], in_=ap["cb4_"][:]), after=attn_done)
    lbo = p.y(lambda e: e.dma_start(out=bo_row[:], in_=ap["bo_"][:]))
    cbts = []
    for (name, dst, n) in (("cb1r", cb1T, 8), ("cb2r", cb2T, 4), ("cb3r", cb3T, 2)):
        lc_ = p.y(lambda e, name=name, n=n: e.dma_start(out=cbr[0:n, :], in_=ap[name][:]),
                  after=takew("cbr"))
        tt = p.t(lambda e, n=n: e.transpose(ps_d[0:128, 0:n], cbr[0:n, :], identf[0:n, 0:n]),
                 after=[lc_, attn_done[0]] + pop_bank("d"))
        use("cbr", tt)
        ct_ = p.v(lambda e, dst=dst, n=n: e.tensor_copy(dst[:, 0:n], ps_d[0:128, 0:n]), after=[tt])
        bank_free["d"] = ct_
        cbts.append(ct_)
    bob1 = p.t(lambda e: e.matmul(ps_a[0][:, 0:512], ones2[0:1, :], bo_row[0:1, 0:512],
                                  start=True, stop=True), after=[lbo, attn_done[0]] + pop_bank("a0"))
    bob1c = p.v(lambda e: e.tensor_copy(boB[:, 0:512], ps_a[0][:, 0:512]), after=[bob1])
    bank_free["a0"] = bob1c
    bob2 = p.t(lambda e: e.matmul(ps_a[1][:, 0:512], ones2[0:1, :], bo_row[0:1, 512:1024],
                                  start=True, stop=True), after=[lbo] + pop_bank("a1"))
    bob2c = p.v(lambda e: e.tensor_copy(boB[:, 512:1024], ps_a[1][:, 0:512]), after=[bob2])
    bank_free["a1"] = bob2c

    for i2 in range(NT_OWN):
        i = i2 % 2
        ld = p.y(lambda e, i2=i2, i=i: e.dma_start(out=fxb_t[i][:], in_=ap["x1q"][i2 * 128:(i2 + 1) * 128, :]),
                 after=(attn_done if i2 < 2 else []) + takew(f"fxb{i}"))
        xt = ln_chain(ld, fxb_t[i], out_tm[i2], fst_t[i], extra_after=takew(f"fln{i}"))
        use(f"fxb{i}", xt)
        use(f"fln{i}", xt)
        use(f"otm{i2}", xt)

    for cyc in range(3):
        for c4 in range(4):
            cps = []
            for sl in range(4):
                i2 = c4 * 4 + sl
                i = i2 % 2
                cst = p.v(lambda e, i2=i2, i=i: e.tensor_copy(obf_t[i][:], out_tm[i2][:]),
                          after=takew(f"otm{i2}")[-1:] + takew(f"obf{i}"))
                use(f"otm{i2}", cst)
                trs = None
                for cc in range(8):
                    trs = p.t(lambda e, cc=cc, i=i: e.transpose(
                        ps_tr[:, cc * 128:(cc + 1) * 128], obf_t[i][:, cc * 128:(cc + 1) * 128], ident[:]),
                        after=([cst] + pop_bank("tr") if cc == 0 else []), inc=(1 if cc == 7 else 0))
                use(f"obf{i}", trs)
                cpy = None
                for cc in range(8):
                    cpy = p.v(lambda e, cc=cc, sl=sl: e.tensor_copy(
                        out_fm[:, cc * 512 + sl * 128: cc * 512 + sl * 128 + 128],
                        ps_tr[:, cc * 128:(cc + 1) * 128]),
                        after=[trs] + (takew(f"of{sl}") if cc == 0 else []))
                bank_free["tr"] = cpy
                cps.append(cpy)
            h1t = []
            for opt in range(8):
                pa = ps_a[opt % 2]
                haft = cps + [lcw1, cbts[0]] + pop_bank(f"a{opt % 2}")
                hmm = None
                for cc in range(8):
                    hmm = p.t(lambda e, cc=cc, opt=opt, pa=pa: e.matmul(
                        pa[:, 0:512], cw1_sb[:, cc * 1024 + opt * 128: cc * 1024 + opt * 128 + 128],
                        out_fm[:, cc * 512:(cc + 1) * 512], start=(cc == 0), stop=(cc == 7)),
                        after=(haft if cc == 0 else []), inc=(1 if cc == 7 else 0))
                for sl in range(4):
                    use(f"of{sl}", hmm)
                he = p.s(lambda e, opt=opt, pa=pa: e.activation(
                    out=h1_fm[:, opt * 512:(opt + 1) * 512], in_=pa[:, 0:512], func=AF.Gelu,
                    bias=cb1T[:, opt:opt + 1]), after=[hmm] + takew(f"h1_{opt}"))
                bank_free[f"a{opt % 2}"] = he
                h1t.append(he)
            h2t = []
            for opt in range(4):
                pa = ps_b[opt % 2]
                haft = h1t + [lcw2, cbts[1]] + pop_bank(f"b{opt % 2}")
                hmm = None
                for cc in range(8):
                    hmm = p.t(lambda e, cc=cc, opt=opt, pa=pa: e.matmul(
                        pa[:, 0:512], cw2_sb[:, cc * 512 + opt * 128: cc * 512 + opt * 128 + 128],
                        h1_fm[:, cc * 512:(cc + 1) * 512], start=(cc == 0), stop=(cc == 7)),
                        after=(haft if cc == 0 else []), inc=(1 if cc == 7 else 0))
                for oo in range(8):
                    use(f"h1_{oo}", hmm)
                he = p.s(lambda e, opt=opt, pa=pa: e.activation(
                    out=h2_fm[:, opt * 512:(opt + 1) * 512], in_=pa[:, 0:512], func=AF.Gelu,
                    bias=cb2T[:, opt:opt + 1]), after=[hmm] + takew(f"h2_{opt}"))
                bank_free[f"b{opt % 2}"] = he
                h2t.append(he)
            h3t = []
            for opt in range(2):
                pa = ps_c[opt % 2]
                haft = h2t + [lcw3, cbts[2]] + pop_bank(f"c{opt % 2}")
                hmm = None
                for cc in range(4):
                    hmm = p.t(lambda e, cc=cc, opt=opt, pa=pa: e.matmul(
                        pa[:, 0:512], cw3_sb[:, cc * 256 + opt * 128: cc * 256 + opt * 128 + 128],
                        h2_fm[:, cc * 512:(cc + 1) * 512], start=(cc == 0), stop=(cc == 3)),
                        after=(haft if cc == 0 else []), inc=(1 if cc == 3 else 0))
                for oo in range(4):
                    use(f"h2_{oo}", hmm)
                he = p.s(lambda e, opt=opt, pa=pa: e.activation(
                    out=h3_fm[:, opt * 512:(opt + 1) * 512], in_=pa[:, 0:512], func=AF.Gelu,
                    bias=cb3T[:, opt:opt + 1]), after=[hmm] + takew(f"h3_{opt}"))
                bank_free[f"c{opt % 2}"] = he
                h3t.append(he)
            tmm = None
            taft = h3t + [lcw4] + pop_bank("d")
            for cc in range(2):
                tmm = p.t(lambda e, cc=cc: e.matmul(
                    ps_d[0:1, 0:512], cw4_sb[:, cc:cc + 1], h3_fm[:, cc * 512:(cc + 1) * 512],
                    start=(cc == 0), stop=(cc == 1)),
                    after=(taft if cc == 0 else []), inc=(1 if cc == 1 else 0))
            use("h3_0", tmm); use("h3_1", tmm)
            tf1 = p.v(lambda e: e.tensor_single_scalar(tfrow[:], ps_d[0:1, 0:512], cb4_sb[0:1, 0:1], ALU.add),
                      after=[tmm, lcb4] + takew("tfrow"))
            bank_free["d"] = tf1
            tf2 = p.s(lambda e: e.activation(out=tfrow[:], in_=tfrow[:], func=AF.Sigmoid), after=[tf1])
            cr1 = p.v(lambda e: e.tensor_scalar_mul(crow[:], tfrow[:], 0.3), after=[tf2, gf_done] + takew("crow"))
            if dbg and cyc == 0 and c4 == 0:
                d_1 = p.y(lambda e: e.dma_start(out=dtf[2:3, :], in_=crow[:]), after=[cr1])
            else:
                d_1 = cr1
            cr2 = p.v(lambda e: e.tensor_single_scalar(crow[:], crow[:], gpar[0:1, 1:2], ALU.add), after=[d_1])
            if dbg and cyc == 0 and c4 == 0:
                d_2 = p.y(lambda e: e.dma_start(out=dtf[3:4, :], in_=crow[:]), after=[cr2])
            else:
                d_2 = cr2
            cr3 = p.v(lambda e: e.tensor_single_scalar(crow[:], crow[:], gpar[0:1, 2:3], ALU.mult), after=[d_2])
            if dbg and cyc == 0 and c4 == 0:
                d_3 = p.y(lambda e: e.dma_start(out=dtf[4:5, :], in_=crow[:]), after=[cr3])
            else:
                d_3 = cr3
            cr4 = p.v(lambda e: e.tensor_scalar_add(crow[:], crow[:], 0.1), after=[d_3])
            cr5 = p.v(lambda e: e.tensor_scalar_mul(cnrow[:], crow[:], -1.0), after=[cr4] + takew("cnrow"))
            use("tfrow", cr1)
            if dbg and cyc == 0 and c4 == 0:
                dtf1 = p.y(lambda e: e.dma_start(out=dtf[0:1, :], in_=tfrow[:]), after=[tf2])
                use("tfrow", dtf1)
                dtf2 = p.y(lambda e: e.dma_start(out=dtf[1:2, :], in_=crow[:]), after=[cr5])
                use("crow", dtf2)
            for sl in range(4):
                i2 = c4 * 4 + sl
                tt1 = p.t(lambda e, sl=sl: e.transpose(ps_d[0:128, 0:1], crow[0:1, sl * 128:(sl + 1) * 128],
                                                       identf[0:1, 0:1]), after=[cr5] + pop_bank("d"))
                use("crow", tt1)
                tc1 = p.v(lambda e: e.tensor_copy(cT[:, 0:1], ps_d[0:128, 0:1]), after=[tt1] + takew("cT"))
                tt2 = p.t(lambda e, sl=sl: e.transpose(ps_d[0:128, 1:2], cnrow[0:1, sl * 128:(sl + 1) * 128],
                                                       identf[0:1, 0:1]), after=[tc1])
                use("cnrow", tt2)
                tc2 = p.v(lambda e: e.tensor_copy(cT[:, 1:2], ps_d[0:128, 1:2]), after=[tt2])
                bank_free["d"] = tc2
                ot = out_tm[i2]
                last = takew(f"otm{i2}")[-1:]
                u1 = p.v(lambda e, ot=ot: e.tensor_copy(tq_sb[:], ot[:, 0:256]), after=[tc2] + last + takew("tq"))
                u2 = p.v(lambda e, ot=ot: e.scalar_tensor_tensor(
                    ot[:, 0:256], ot[:, 512:768], cT[:, 0:1], ot[:, 0:256], ALU.mult, ALU.add), after=[u1])
                u3 = p.v(lambda e, ot=ot: e.scalar_tensor_tensor(
                    ot[:, 512:768], tq_sb[:], cT[:, 1:2], ot[:, 512:768], ALU.mult, ALU.add), after=[u2])
                u4 = p.v(lambda e, ot=ot: e.tensor_copy(tq_sb[:], ot[:, 256:512]), after=[u3])
                u5 = p.v(lambda e, ot=ot: e.scalar_tensor_tensor(
                    ot[:, 256:512], ot[:, 768:1024], cT[:, 1:2], ot[:, 256:512], ALU.mult, ALU.add), after=[u4])
                u6 = p.v(lambda e, ot=ot: e.scalar_tensor_tensor(
                    ot[:, 768:1024], tq_sb[:], cT[:, 0:1], ot[:, 768:1024], ALU.mult, ALU.add), after=[u5])
                use("tq", u6)
                use("cT", u6)
                use(f"otm{i2}", u6)

    if dbg:
        for i2 in range(NT_OWN):
            dfo = p.y(lambda e, i2=i2: e.dma_start(out=dffn[i2 * 128:(i2 + 1) * 128, :], in_=out_tm[i2][:]),
                      after=takew(f"otm{i2}")[-1:])
            use(f"otm{i2}", dfo)
    if dbg:
        pass
    for i2 in range(NT_OWN):
        i = i2 % 2
        ldx = p.y(lambda e, i2=i2, i=i: e.dma_start(out=facc[i][:], in_=ap["x1q"][i2 * 128:(i2 + 1) * 128, :]),
                  after=takew(f"facc{i}"))
        lda = p.y(lambda e, i2=i2, i=i: e.dma_start(out=fatt[i][:], in_=att1_d[i2 * 128:(i2 + 1) * 128, :]),
                  after=(attn_done if i2 == 0 else []) + takew(f"fatt{i}"))
        a1 = p.v(lambda e, i=i: e.tensor_add(facc[i][:], facc[i][:], fatt[i][:]), after=[ldx, lda])
        a2 = p.v(lambda e, i=i: e.tensor_add(facc[i][:], facc[i][:], boB[:]), after=[a1, bob1c, bob2c])
        a3 = p.v(lambda e, i2=i2, i=i: e.tensor_add(facc[i][:], facc[i][:], out_tm[i2][:]),
                 after=[a2] + takew(f"otm{i2}")[-1:])
        use(f"fatt{i}", a1)
        st = p.y(lambda e, i2=i2, i=i: e.dma_start(out=y1[i2 * 128:(i2 + 1) * 128, :], in_=facc[i][:]),
                 after=[a3])
        use(f"facc{i}", st)
    p.op("y", lambda e: e.nop(), after=[p.now("y")], inc=0)

    sB.close()
    _emit(nc, p, sems)
    ctx.close()
    return nc



def _prep_host(ins):
    rc, rs = rope_tables()
    shared = {
        "rc": rc, "rs": rs,
        "wqT": np.ascontiguousarray(ins["L0_attn_W_q"].T),
        "wkT": np.ascontiguousarray(ins["L0_attn_W_k"].T),
        "wvT": np.ascontiguousarray(ins["L0_attn_W_v"].T),
        "woT": np.ascontiguousarray(ins["L0_attn_W_o"].T),
        "wrT": np.ascontiguousarray(ins["L0_ffn_Wr"].T),
        "br_": np.ascontiguousarray(ins["L0_ffn_br"][None, :]),
        "pw2": (2.0 ** -np.arange(8, dtype=np.float32))[None, :].astype(np.float32),
        "w1T": np.ascontiguousarray(ins["L0_ffn_W1"].transpose(0, 2, 1)),
        "b1_": np.ascontiguousarray(ins["L0_ffn_b1"]),
        "w2T": np.ascontiguousarray(ins["L0_ffn_W2"].transpose(0, 2, 1)),
        "b2_": np.ascontiguousarray(ins["L0_ffn_b2"]),
        "idm": np.eye(128, dtype=np.float32),
        "un8": np.kron(np.eye(8, dtype=np.float32), np.ones((1, 128), np.float32)),
    }
    in_maps = []
    own_rows = {}
    for core in range(8):
        b, par = core // 2, core % 2
        tiles = np.arange(par, NT_ABS, 2)
        rows = (tiles[:, None] * 128 + np.arange(128)[None, :]).ravel()
        own_rows[core] = rows
        cmask, lmask = host_constants(par)
        m = dict(shared)
        m["xb"] = np.ascontiguousarray(ins["L0_attn_X"][b])
        m["xq"] = np.ascontiguousarray(ins["L0_attn_X"][b][rows])
        m["rcq"] = np.ascontiguousarray(rc[rows])
        m["rsq"] = np.ascontiguousarray(rs[rows])
        m["cmask"] = cmask
        in_maps.append(m)
    return in_maps, own_rows


def _prep_host_l1(ins):
    shared = {
        "l1qT": np.ascontiguousarray(ins["L1_attn_Wq"].T),
        "l1kT": np.ascontiguousarray(ins["L1_attn_Wk"].T),
        "l1vT": np.ascontiguousarray(ins["L1_attn_Wv"].T),
        "l1oT": np.ascontiguousarray(ins["L1_attn_Wo"].T),
        "bq16": np.ascontiguousarray(ins["L1_attn_bq"].reshape(16, 64)),
        "bk16": np.ascontiguousarray(ins["L1_attn_bk"].reshape(16, 64)),
        "bv_": np.ascontiguousarray(ins["L1_attn_bv"][None, :]),
        "bo_": np.ascontiguousarray(ins["L1_attn_bo"][None, :]),
        "wsT": np.ascontiguousarray(ins["L1_attn_Ws"].T),
        "bs16": np.ascontiguousarray(ins["L1_attn_bs"][:, None]),
        "cw1": np.ascontiguousarray(ins["L1_ffn_cn_w1"]),
        "cw2": np.ascontiguousarray(ins["L1_ffn_cn_w2"]),
        "cw3": np.ascontiguousarray(ins["L1_ffn_cn_w3"]),
        "cw4": np.ascontiguousarray(ins["L1_ffn_cn_w4"]),
        "cb1r": np.ascontiguousarray(ins["L1_ffn_cn_b1"].reshape(8, 128)),
        "cb2r": np.ascontiguousarray(ins["L1_ffn_cn_b2"].reshape(4, 128)),
        "cb3r": np.ascontiguousarray(ins["L1_ffn_cn_b3"].reshape(2, 128)),
        "cb4_": np.ascontiguousarray(ins["L1_ffn_cn_b4"].reshape(1, 1)),
        "gw1": np.ascontiguousarray(ins["L1_ffn_gc_w1"]),
        "gw2": np.ascontiguousarray(ins["L1_ffn_gc_w2"]),
        "gw3": np.ascontiguousarray(ins["L1_ffn_gc_w3"]),
        "gb1_": np.ascontiguousarray(ins["L1_ffn_gc_b1"][None, :]),
        "gb2_": np.ascontiguousarray(ins["L1_ffn_gc_b2"][None, :]),
        "gb3_": np.ascontiguousarray(ins["L1_ffn_gc_b3"].reshape(1, 1)),
        "ar_": np.ascontiguousarray(ins["L1_ffn_adaptive_range"].reshape(1, 1)),
        "idm": np.eye(128, dtype=np.float32),
    }
    return shared


def kernel(**ins):
    ins = {k: np.asarray(v) for k, v in ins.items()}
    in_maps, own_rows = _prep_host(ins)
    nc0 = build_l0()
    res0 = run_bass_kernel_spmd(nc0, in_maps, core_ids=list(range(8)))
    x1 = np.zeros((B, S, C), dtype=np.float32)
    for core in range(8):
        b = core // 2
        x1[b][own_rows[core]] = res0.results[core]["y0"]
    sh1 = _prep_host_l1(ins)
    in_maps1 = []
    for core in range(8):
        b, par = core // 2, core % 2
        m = dict(sh1)
        m["x1r"] = np.ascontiguousarray(
            np.concatenate([x1[b][own_rows[2 * b + 0]], x1[b][own_rows[2 * b + 1]]], axis=0))
        m["x1q"] = np.ascontiguousarray(x1[b][own_rows[core]])
        m["lmask"] = host_constants(par)[1]
        in_maps1.append(m)
    nc1 = build_l1()
    res1 = run_bass_kernel_spmd(nc1, in_maps1, core_ids=list(range(8)))
    out = np.zeros((B, S, C), dtype=np.float32)
    for core in range(8):
        b = core // 2
        out[b][own_rows[core]] = res1.results[core]["y1"]
    return out
